# version 01
# baseline (speedup 1.0000x reference)
import contextlib
import math
import numpy as np
import concourse.bass as bass
import concourse.mybir as mybir
from concourse.bass_utils import run_bass_kernel_spmd

F32 = mybir.dt.float32
BF16 = mybir.dt.bfloat16
AF = mybir.ActivationFunctionType
ALU = mybir.AluOpType
AX = mybir.AxisListType

NCORE = 8
T = 2048
D = 1024
NC = D // 128
NT = T // 128

FUSED = False
LAYER_GROUPS = [[0], [1], [2], [3]]

ENGS = ("pe", "act", "dve", "pool", "sp")
_BODY = {"pe": "tensor", "act": "scalar", "dve": "vector", "pool": "gpsimd", "sp": "sync"}
_OWN = {"pe": "pe", "act": None, "dve": None, "pool": None, "sp": None}
_AUTO = {"dve": "dve", "act": "act", "pool": "pool"}


class Prog:
    def __init__(self, nc):
        self.nc = nc
        self.stack = contextlib.ExitStack()
        self.sem = {}
        self.cnt = {}
        self.dma_chans = set()
        self.waited = {e: {} for e in ENGS}
        self.last_tok = {}
        self.pending = {e: [] for e in ENGS}
        for s in ("pe", "act", "dve", "pool", "sp2", "cc"):
            self.sem[s] = self.stack.enter_context(nc.semaphore(s))
            self.cnt[s] = 0

    def chan(self, name):
        if name not in self.sem:
            self.sem[name] = self.stack.enter_context(self.nc.semaphore(name))
            self.cnt[name] = 0
            self.dma_chans.add(name)
        return name

    def emit(self, eng, fn, waits=(), inc=None, n=1):
        own = _OWN[eng]
        auto = _AUTO.get(eng)
        if auto is not None:
            waits = list(waits) + [self.last_tok.get(eng)]
            if inc is None:
                inc = auto
        elif eng == "pe" and inc is None:
            inc = "pe"
        rw = []
        for t in waits:
            if t is None:
                continue
            s, v = t
            if s == own:
                continue
            assert v <= self.cnt[s], f"wait on future token {s}={v} (cnt={self.cnt[s]})"
            if s in self.dma_chans:
                assert v == self.cnt[s], \
                    f"stale DMA-channel wait {s}={v} (cnt={self.cnt[s]}): needs its own channel"
            if v > self.waited[eng].get(s, 0):
                rw.append((s, v))
                self.waited[eng][s] = v
        tok = None
        if inc is not None:
            if inc not in self.sem:
                self.chan(inc) if n == 16 else None
            self.cnt[inc] += n
            tok = (inc, self.cnt[inc])
        if auto is not None:
            self.last_tok[eng] = tok
        self.pending[eng].append((rw, fn, inc, n))
        return tok

    def dma(self, eng, fn, ch, waits=()):
        return self.emit(eng, fn, waits=waits, inc=self.chan(ch), n=16)

    _BSEM = {"pe": "pe", "act": "act", "dve": "dve", "pool": "pool"}

    def barrier(self, engs=ENGS):
        toks = [(s, self.cnt[s]) for e, s in self._BSEM.items()
                if (engs is ENGS or e in engs) and self.cnt[s] > 0]
        for e in engs:
            self.emit(e, lambda h: h.nop(), waits=toks)

    def flush(self):
        if not any(self.pending[e] for e in ENGS):
            return
        with self.nc.Block() as blk:
            for eng in ENGS:
                items = self.pending[eng]
                if not items:
                    continue
                self.pending[eng] = []

                def body(h, items=items):
                    for (rw, fn, inc, n) in items:
                        for (s, v) in rw:
                            h.wait_ge(self.sem[s], v)
                        ins = fn(h)
                        if inc is not None:
                            ins.then_inc(self.sem[inc], n)
                getattr(blk, _BODY[eng])(body)


IN_SHAPES = {
    "L0_attn_w_qkv": (1024, 3072), "L0_attn_proj_matrix": (16, 64, 256),
    "L0_attn_w_out": (1024, 1024), "L0_attn_b_out": (1024,),
    "L0_ffn_w1_w": (4096, 1024), "L0_ffn_w1_s": (4096,),
    "L0_ffn_w2_w": (4096, 1024), "L0_ffn_w2_s": (4096,),
    "L0_ffn_h1_w": (4096, 4096), "L0_ffn_h1_s": (4096,),
    "L0_ffn_h2_w": (4096, 4096), "L0_ffn_h2_s": (4096,),
    "L0_ffn_w3_w": (1024, 4096), "L0_ffn_w3_s": (1024,),
    "L1_attn_W_K": (16, 64, 1024), "L1_attn_W_Q": (16, 64, 1024),
    "L1_attn_W_V": (16, 64, 1024), "L1_attn_W_O": (1024, 1024),
    "L1_ffn_gate_w": (8, 1024), "L1_ffn_w1": (8, 1024, 1024), "L1_ffn_w2": (8, 1024, 1024),
    "L2_attn_Wq": (1024, 1024), "L2_attn_bq": (1024,),
    "L2_ffn_weight": (1024, 1024), "L2_ffn_gate_w": (8, 1024),
    "L2_ffn_lora_U": (8, 16, 1024), "L2_ffn_lora_V": (8, 1024, 16),
    "L3_attn_Wq": (1024, 512), "L3_attn_Wk": (256, 512), "L3_attn_Wv": (256, 512),
    "L3_attn_Wo": (1024, 1024), "L3_attn_q_phase_angle": (16,),
    "L3_attn_alibi_log_slopes": (16,),
    "L3_ffn_gate_w": (1024, 8), "L3_ffn_w1": (8, 1024, 4096), "L3_ffn_b1": (8, 4096),
    "L3_ffn_w2": (8, 4096, 1024), "L3_ffn_b2": (8, 1024),
    "L3_ffn_gamma": (1024,), "L3_ffn_beta": (1024,),
}
LAYER_INPUTS = {li: [k for k in IN_SHAPES if k.startswith(f"L{li}_")] for li in range(4)}


class Build:
    def __init__(self, layers):
        self.layers = layers
        nc = bass.Bass("TRN2", target_bir_lowering=False, disable_frame_to_traceback=True)
        self.nc = nc
        self.P = Prog(nc)
        st = self.P.stack
        self.ins = {}
        for li in layers:
            for k in LAYER_INPUTS[li]:
                self.ins[k] = nc.dram_tensor(k, list(IN_SHAPES[k]), F32, kind="ExternalInput").ap()
        self.x_in = nc.dram_tensor("x_in", [T, D], F32, kind="ExternalInput").ap()
        self.x_out = nc.dram_tensor("x_out", [T, D], F32, kind="ExternalOutput").ap()

        def sbuf(name, shape, dtype=F32):
            return st.enter_context(nc.sbuf_tensor(name, list(shape), dtype))
        self.gsbuf = sbuf
        self.x_fm = sbuf("x_fm", (128, NC * T))
        self.xn_bf = sbuf("xn_bf", (128, NC * T), BF16)
        self.ones_r32 = sbuf("ones_r32", (1, 128))
        self.ones_rbf = sbuf("ones_rbf", (1, 128), BF16)
        self.ones_c32 = sbuf("ones_c32", (128, 1))
        self.ones_cbf = sbuf("ones_cbf", (128, 1), BF16)
        self.ident32 = sbuf("ident32", (128, 128))
        self.identbf = sbuf("identbf", (128, 128), BF16)
        self.murow = self.rsrow = self.srow = self.srow2 = None
        self.mu_t = sbuf("mu_t", (128, NT))
        self.rs_t = sbuf("rs_t", (128, NT))
        self.prep_toks = {}
        self.prep_aps = {}
        self._init_consts()

    def _init_consts(self):
        P = self.P
        tok = [None]
        def g(fn):
            tok[0] = P.emit("pool", fn, waits=[tok[0]], inc="pool")
            return tok[0]
        g(lambda h: h.memset(self.ones_r32[:, :], 1.0))
        g(lambda h: h.memset(self.ones_rbf[:, :], 1.0))
        g(lambda h: h.memset(self.ones_c32[:, :], 1.0))
        g(lambda h: h.memset(self.ones_cbf[:, :], 1.0))
        g(lambda h: h.memset(self.ident32[:, :], 0.0))
        g(lambda h: h.affine_select(self.ident32[:, :], self.ident32[:, :], pattern=[[-1, 128]],
                                    compare_op=ALU.not_equal, fill=1.0, base=0, channel_multiplier=1))
        g(lambda h: h.memset(self.identbf[:, :], 0.0))
        self.consts_tok = g(lambda h: h.affine_select(self.identbf[:, :], self.identbf[:, :],
                                                      pattern=[[-1, 128]], compare_op=ALU.not_equal,
                                                      fill=1.0, base=0, channel_multiplier=1))

    def prep_cast(self, name, shape2d=None):
        if name in self.prep_toks:
            return self.prep_aps[name], self.prep_toks[name]
        P, nc = self.P, self.nc
        src = self.ins[name]
        sh = IN_SHAPES[name]
        flat = (int(np.prod(sh[:-1])), sh[-1]) if shape2d is None else shape2d
        dst = nc.dram_tensor(name + "_bf", list(flat), BF16).ap()
        srcf = src.rearrange(" ".join(f"a{i}" for i in range(len(sh))) + " -> (" +
                             " ".join(f"a{i}" for i in range(len(sh) - 1)) + f") a{len(sh)-1}") \
            if len(sh) > 2 else src
        if len(sh) == 1:
            srcf = src
        rows_per = max(1, (8 << 20) // (flat[1] * 4))
        tok = None
        r = 0
        while r < flat[0]:
            rr = min(rows_per, flat[0] - r)
            tok = P.dma("pool", lambda h, r=r, rr=rr: h.dma_start(dst[r:r + rr, :], srcf[r:r + rr, :]),
                        "pc_" + name)
            r += rr
        self.prep_toks[name] = tok
        self.prep_aps[name] = dst
        return dst, tok

    def prep_transposed_dram(self, name, bounce):
        key = name + "_T"
        if key in self.prep_toks:
            return self.prep_aps[key], self.prep_toks[key]
        P, nc = self.P, self.nc
        sh = IN_SHAPES[name]
        O, I = int(np.prod(sh[:-1])), sh[-1]
        NBo, NBc = O // 128, I // 128
        wbf, ctok = self.prep_cast(name)
        wTS = nc.dram_tensor(key, [128, NBo * NBc * 128], BF16).ap()
        wTS4 = wTS.rearrange("p (ob c o) -> p ob c o", ob=NBo, c=NBc, o=128)
        if not hasattr(self, "_bounce_busy"):
            self._bounce_busy = [None, None]
        last_on = {0: None, 1: None}
        for ic in range(NBc):
            bi = ic % 2
            ttok = P.dma("sp", lambda h, ic=ic, bi=bi: h.dma_start(
                bounce[bi][:, 0:O], wbf[:, ic * 128:(ic + 1) * 128], transpose=True),
                "pt_" + name, waits=[ctok, self._bounce_busy[bi]])
            OBG = max(1, 1024 // NBc)
            stok = None
            for ob0 in range(0, NBo, OBG):
                obn = min(OBG, NBo - ob0)
                stok = P.dma("sp", lambda h, ic=ic, bi=bi, ob0=ob0, obn=obn: h.dma_start(
                    wTS4[:, ob0:ob0 + obn, ic, :],
                    bounce[bi][:, ob0 * 128:(ob0 + obn) * 128].rearrange(
                        "p (ob o) -> p ob o", o=128)),
                    f"ps_{name}{bi}", waits=[ttok])
            self._bounce_busy[bi] = stok
            last_on[bi] = stok
        toks = [t for t in (last_on[0], last_on[1]) if t is not None]
        self.prep_toks[key] = toks
        self.prep_aps[key] = wTS
        return wTS, toks

    def load_x(self):
        P, nc = self.P, self.nc
        with contextlib.ExitStack() as st:
            xtm = st.enter_context(nc.sbuf_tensor("xtm_load", [128, NT * D], F32))
            pst = [st.enter_context(nc.psum_tensor(f"pst_load{i}", [128, 512], F32)) for i in range(2)]
            ldtok = None
            for t in range(NT):
                ldtok = P.dma(
                    "sp", lambda h, t=t: h.dma_start(xtm[:, t * D:(t + 1) * D],
                                                     self.x_in[t * 128:(t + 1) * 128, :]),
                    "lx")
            evtoks = []
            for c in range(NC):
                for tg in range(NT // 4):
                    ps = pst[tg % 2]
                    mmt = None
                    for j in range(4):
                        t = tg * 4 + j
                        w = [ldtok, self.consts_tok,
                             evtoks[-2] if len(evtoks) >= 2 else None]
                        mmt = P.emit("pe", lambda h, ps=ps, j=j, c=c, t=t:
                                     h.transpose(ps[:, j * 128:(j + 1) * 128],
                                                 xtm[:, t * D + c * 128: t * D + (c + 1) * 128],
                                                 self.ident32[:, :]),
                                     waits=w, inc="pe")
                    evtoks.append(P.emit(
                        "act", lambda h, ps=ps, c=c, tg=tg:
                        h.activation(self.x_fm[:, c * T + tg * 512: c * T + (tg + 1) * 512],
                                     ps[:, :], AF.Copy),
                        waits=[mmt], inc="act"))
            P.flush()

    def store_x(self):
        P, nc = self.P, self.nc
        with contextlib.ExitStack() as st:
            xtm = st.enter_context(nc.sbuf_tensor("xtm_store", [128, 2 * D], F32))
            pst = [st.enter_context(nc.psum_tensor(f"pst_store{i}", [128, 512], F32)) for i in range(2)]
            P.barrier()
            dtoks = {0: None, 1: None}
            ev_last = {0: None, 1: None}
            for t in range(NT):
                evtoks = []
                for cg in range(2):
                    ps = pst[cg]
                    mmt = None
                    for j in range(4):
                        c = cg * 4 + j
                        w = [dtoks[t % 2], ev_last[cg]]
                        mmt = P.emit("pe", lambda h, ps=ps, j=j, c=c, t=t:
                                     h.transpose(ps[:, j * 128:(j + 1) * 128],
                                                 self.x_fm[:, c * T + t * 128: c * T + (t + 1) * 128],
                                                 self.ident32[:, :]),
                                     waits=w, inc="pe")
                    ev_last[cg] = P.emit(
                        "act", lambda h, ps=ps, cg=cg, t=t:
                        h.activation(xtm[:, (t % 2) * D + cg * 512:(t % 2) * D + (cg + 1) * 512],
                                     ps[:, :], AF.Copy),
                        waits=[mmt], inc="act")
                    evtoks.append(ev_last[cg])
                dtoks[t % 2] = P.dma(
                    "sp", lambda h, t=t: h.dma_start(self.x_out[t * 128:(t + 1) * 128, :],
                                                     xtm[:, (t % 2) * D:(t % 2 + 1) * D]),
                    "sxa" if t % 2 == 0 else "sxb", waits=evtoks)
            P.emit("sp", lambda h: h.nop(), waits=[dtoks[0], dtoks[1]])
            P.flush()

    def layernorm(self, want_tok_stats=False, rows_stack=None, l3=False):
        P, nc = self.P, self.nc
        with contextlib.ExitStack() as st:
            murow_t = (rows_stack or st).enter_context(nc.sbuf_tensor("ln_murow", [1, T], F32))
            if l3:
                self.r2row = rows_stack.enter_context(
                    nc.sbuf_tensor("ln_r2row", [1, T], F32))[:, :]
                self.rr_t = rows_stack.enter_context(nc.sbuf_tensor("ln_rrt", [128, NT], F32))
            rows = st.enter_context(nc.sbuf_tensor("ln_rows", [1, 3 * T], F32))
            if l3:
                self.rrrow = st.enter_context(nc.sbuf_tensor("ln_rrrow", [1, T], F32))[:, :]
            self.murow = murow_t[:, :]
            self.rsrow = rows[:, 0 * T:1 * T]
            self.srow = rows[:, 1 * T:2 * T]
            self.srow2 = rows[:, 2 * T:3 * T]
            tmpA = st.enter_context(nc.sbuf_tensor("ln_tmpA", [128, T], F32))
            tmpB = st.enter_context(nc.sbuf_tensor("ln_tmpB", [128, T], F32))
            MU_B = st.enter_context(nc.sbuf_tensor("ln_MUB", [128, T], F32))
            RS_B = st.enter_context(nc.sbuf_tensor("ln_RSB", [128, T], F32))
            stp = contextlib.ExitStack()
            S1p = stp.enter_context(nc.psum_tensor("S1p", [1, T], F32))
            S2p = stp.enter_context(nc.psum_tensor("S2p", [1, T], F32))
            P.barrier()
            s1_toks = []
            for j in range(4):
                mmt = None
                for c in range(NC):
                    mmt = P.emit("pe", lambda h, c=c, j=j:
                                 h.matmul(S1p[:, j * 512:(j + 1) * 512], self.ones_c32[:, :],
                                          self.x_fm[:, c * T + j * 512: c * T + (j + 1) * 512],
                                          start=(c == 0), stop=(c == NC - 1)),
                                 waits=[self.consts_tok], inc="pe")
                s1_toks.append(mmt)
            sq_toks = {}
            prev_mm = {}
            last_s2 = None
            for c in range(NC):
                tmp = tmpA if c % 2 == 0 else tmpB
                sq_toks[c] = P.emit("act", lambda h, c=c, tmp=tmp:
                                    h.activation(tmp[:, :], self.x_fm[:, c * T:(c + 1) * T], AF.Square),
                                    waits=[prev_mm.get(c - 2)], inc="act")
                mmt = None
                for j in range(4):
                    mmt = P.emit("pe", lambda h, c=c, j=j, tmp=tmp:
                                 h.matmul(S2p[:, j * 512:(j + 1) * 512], self.ones_c32[:, :],
                                          tmp[:, j * 512:(j + 1) * 512],
                                          start=(c == 0), stop=(c == NC - 1)),
                                 waits=[sq_toks[c]], inc="pe")
                prev_mm[c] = mmt
                last_s2 = mmt
            t_mu = P.emit("dve", lambda h: h.tensor_scalar_mul(self.murow, S1p[:, :], 1.0 / D),
                          waits=s1_toks, inc="dve")
            t_m2 = P.emit("dve", lambda h: h.tensor_scalar_mul(self.srow, S2p[:, :], 1.0 / D),
                   waits=[last_s2], inc="dve")
            P.emit("dve", lambda h: h.tensor_mul(self.srow2, self.murow, self.murow),
                   inc="dve")
            t_var = P.emit("dve", lambda h: h.tensor_sub(self.srow, self.srow, self.srow2),
                           inc="dve")
            t_var = P.emit("dve", lambda h: h.tensor_scalar_add(self.srow, self.srow, 1e-5),
                           waits=[t_var], inc="dve")
            P.emit("act", lambda h: h.activation(self.srow2, self.srow, AF.Ln),
                   waits=[t_var], inc="act")
            t_rs = P.emit("act", lambda h: h.activation(self.rsrow, self.srow2, AF.Exp, scale=-0.5),
                          inc="act")
            if l3:
                t3 = P.emit("act", lambda h: h.activation(self.r2row, self.srow2, AF.Exp,
                                                          scale=-1.0), inc="act")
                t3 = P.emit("dve", lambda h: h.tensor_scalar(
                    self.r2row, self.r2row, -1e-5, 1.0, ALU.mult, ALU.add),
                    waits=[t3], inc="dve")
                t3 = P.emit("dve", lambda h: h.tensor_scalar_add(self.r2row, self.r2row, 1e-5),
                            inc="dve")
                t3 = P.emit("act", lambda h: h.activation(self.rrrow, self.r2row, AF.Ln),
                            waits=[t3], inc="act")
                t3 = P.emit("act", lambda h: h.activation(self.r2row, self.rrrow, AF.Exp,
                                                          scale=-0.5), inc="act")
                t_rr = P.emit("dve", lambda h: h.tensor_mul(self.rrrow, self.r2row, self.rsrow),
                              waits=[t3, t_rs], inc="dve")
            P.flush()
            stp.close()
            with nc.psum_tensor("bps", [128, 512], F32) as bps, \
                 nc.psum_tensor("bps2", [128, 512], F32) as bps2:
                ev_mu = []
                ev_rs = []
                for j in range(4):
                    ps = bps if j % 2 == 0 else bps2
                    m1 = P.emit("pe", lambda h, j=j, ps=ps:
                                h.matmul(ps[:, :], self.ones_r32[:, :],
                                         self.murow[:, j * 512:(j + 1) * 512], start=True, stop=True),
                                waits=[t_mu, t_m2, ev_mu[-2] if len(ev_mu) >= 2 else None], inc="pe")
                    ev_mu.append(P.emit("act", lambda h, j=j, ps=ps: h.activation(
                        MU_B[:, j * 512:(j + 1) * 512], ps[:, :], AF.Copy), waits=[m1], inc="act"))
                for j in range(4):
                    ps = bps if j % 2 == 0 else bps2
                    m1 = P.emit("pe", lambda h, j=j, ps=ps:
                                h.matmul(ps[:, :], self.ones_r32[:, :],
                                         self.rsrow[:, j * 512:(j + 1) * 512], start=True, stop=True),
                                waits=[t_rs, ev_mu[2 + j] if j < 2 else ev_rs[j - 2]], inc="pe")
                    ev_rs.append(P.emit("act", lambda h, j=j, ps=ps: h.activation(
                        RS_B[:, j * 512:(j + 1) * 512], ps[:, :], AF.Copy), waits=[m1], inc="act"))
                if want_tok_stats:
                    with nc.psum_tensor("tps", [128, 3 * NT], F32) as tps:
                        tt = None
                        for t in range(NT):
                            tt = P.emit("pe", lambda h, t=t:
                                        h.transpose(tps[:, t:t + 1],
                                                    self.murow[:, t * 128:(t + 1) * 128],
                                                    self.ident32[0:1, 0:1]),
                                        waits=[t_mu], inc="pe")
                            tt = P.emit("pe", lambda h, t=t:
                                        h.transpose(tps[:, NT + t:NT + t + 1],
                                                    self.rsrow[:, t * 128:(t + 1) * 128],
                                                    self.ident32[0:1, 0:1]),
                                        waits=[t_rs], inc="pe")
                            if l3:
                                tt = P.emit("pe", lambda h, t=t:
                                            h.transpose(tps[:, 2 * NT + t:2 * NT + t + 1],
                                                        self.rrrow[:, t * 128:(t + 1) * 128],
                                                        self.ident32[0:1, 0:1]),
                                            waits=[t_rr], inc="pe")
                        P.emit("act", lambda h: h.activation(self.mu_t[:, :], tps[:, 0:NT], AF.Copy),
                               waits=[tt], inc="act")
                        P.emit("act", lambda h: h.activation(self.rs_t[:, :], tps[:, NT:2 * NT], AF.Copy),
                               inc="act")
                        if l3:
                            P.emit("act", lambda h: h.activation(self.rr_t[:, :],
                                                                 tps[:, 2 * NT:3 * NT], AF.Copy),
                                   inc="act")
                        P.flush()
                xn_toks = []
                for c in range(NC):
                    ts_ = P.emit("dve", lambda h, c=c:
                                 h.tensor_sub(tmpA[:, :] if c % 2 == 0 else tmpB[:, :],
                                              self.x_fm[:, c * T:(c + 1) * T], MU_B[:, :]),
                                 waits=[ev_mu[-1], ev_rs[-1], prev_mm.get(c)], inc="dve")
                    xn_toks.append(P.emit("dve", lambda h, c=c:
                                          h.tensor_mul(self.xn_bf[:, c * T:(c + 1) * T],
                                                       tmpA[:, :] if c % 2 == 0 else tmpB[:, :],
                                                       RS_B[:, :]),
                                          inc="dve"))
                P.barrier()
                P.flush()
        return xn_toks[-1]


def emit_l0(B):
    P, nc = B.P, B.nc
    with contextlib.ExitStack() as pst:
        bounce = [pst.enter_context(nc.sbuf_tensor(f"l0bounce{i}", [128, 4096], BF16)) for i in range(2)]
        for wname in ("L0_ffn_w1_w", "L0_ffn_w2_w", "L0_ffn_h1_w", "L0_ffn_h2_w", "L0_ffn_w3_w"):
            B.prep_transposed_dram(wname, bounce)
        B.layernorm(want_tok_stats=False)
        emit_l0_attn(B)
        P.barrier(("sp", "pool"))
        P.flush()
    emit_l0_ffn(B)


def emit_l0_attn(B):
    P, nc = B.P, B.nc
    with contextlib.ExitStack() as st:
        def sb(name, shape, dtype=BF16):
            return st.enter_context(nc.sbuf_tensor(name, list(shape), dtype))
        wq = [sb(f"l0_wq{i}", (128, NC * 64)) for i in range(2)]
        wk = [sb(f"l0_wk{i}", (128, NC * 64)) for i in range(2)]
        wv = [sb(f"l0_wv{i}", (128, NC * 64)) for i in range(2)]
        wpj = [sb(f"l0_wpj{i}", (64, 256)) for i in range(2)]
        q_fm = sb("l0_qfm", (64, T))
        k_fm = sb("l0_kfm", (64, T))
        v_tm = sb("l0_vtm", (128, NT * 64))
        qp_fm = sb("l0_qpfm", (128, 2 * T))
        kp_tm = sb("l0_kptm", (128, NT * 256))
        kv_bf = sb("l0_kvbf", (128, 2 * 64))
        ksum_bf = sb("l0_ksum", (128, 2))
        att_all = sb("l0_attall", (128, NC * T))
        zrow = sb("l0_zrow", (1, T), F32)
        fscr = sb("l0_fscr", (128, T), F32)
        wo = [sb(f"l0_wo{i}", (128, 128)) for i in range(2)]
        bout = sb("l0_bout", (128, NC), F32)
        bo_nat = sb("l0_bonat", (NC, 128), F32)

        ps1 = st.enter_context(nc.psum_tensor("l0ps1", [128, 1024], F32))
        ps2 = st.enter_context(nc.psum_tensor("l0ps2", [128, 1024], F32))
        ps_kv = st.enter_context(nc.psum_tensor("l0pskv", [128, 512], F32))
        ps_ks = st.enter_context(nc.psum_tensor("l0psks", [128, 512], F32))
        ps_dn = st.enter_context(nc.psum_tensor("l0psdn", [1, 512], F32))

        w_qkv = B.ins["L0_attn_w_qkv"]
        proj = B.ins["L0_attn_proj_matrix"]
        w_out = B.ins["L0_attn_w_out"]
        b_out = B.ins["L0_attn_b_out"]

        tok = P.dma("pool", lambda h: h.dma_start(
            bo_nat[:, :], b_out.rearrange("(c p) -> c p", p=128)), "gA")
        t1 = P.emit("pe", lambda h: h.transpose(ps_kv[:, 0:NC], bo_nat[:, :], B.ident32[0:NC, 0:NC]),
                    waits=[tok, B.consts_tok], inc="pe")
        bo_tok = P.emit("act", lambda h: h.activation(bout[:, :], ps_kv[:, 0:NC], AF.Copy),
                        waits=[t1], inc="act")

        wfree = {0: None, 1: None}

        def load_head_w(h_, buf):
            ch = "gB" if buf == 0 else "gC"
            tok = None
            first = True
            for (dst, col0) in ((wq[buf], h_ * 64), (wk[buf], D + h_ * 64), (wv[buf], 2 * D + h_ * 64)):
                tok = P.dma("pool", lambda h, dst=dst, col0=col0: h.dma_start(
                    dst[:, :].rearrange("p (c f) -> p c f", f=64),
                    w_qkv[:, col0:col0 + 64].rearrange("(c p) f -> p c f", p=128)),
                    ch, waits=[wfree[buf]] if first else [])
                first = False
            tok = P.dma("pool", lambda h, h_=h_, buf=buf: h.dma_start(
                wpj[buf][:, :], proj[h_, :, :]), ch)
            return [tok]

        pf = load_head_w(0, 0)
        prev_att_done = None
        for h_ in range(16):
            buf = h_ % 2
            cur = pf
            if h_ < 15:
                pf = load_head_w(h_ + 1, (h_ + 1) % 2)
            qk_ev = []
            for (wt, dst, ps) in ((wq[buf], q_fm, ps1), (wk[buf], k_fm, ps2)):
                for jh in range(2):
                    mmt = None
                    for c in range(NC):
                        for j2 in range(2):
                            w = cur + [prev_att_done]
                            if c == 0 and qk_ev:
                                w.append(qk_ev[-1])
                            mmt = P.emit("pe", lambda h, wt=wt, c=c, jh=jh, j2=j2, ps=ps:
                                         h.matmul(ps[0:64, j2 * 512:(j2 + 1) * 512],
                                                  wt[:, c * 64:(c + 1) * 64],
                                                  B.xn_bf[:, c * T + jh * 1024 + j2 * 512:
                                                          c * T + jh * 1024 + (j2 + 1) * 512],
                                                  start=(c == 0), stop=(c == NC - 1)),
                                         waits=w, inc="pe")
                    qk_ev.append(P.emit("act", lambda h, dst=dst, jh=jh, ps=ps: h.activation(
                        dst[:, jh * 1024:(jh + 1) * 1024], ps[0:64, :], AF.Copy),
                        waits=[mmt], inc="act"))
            v_ev = []
            for t in range(NT):
                mmt = None
                for c in range(NC):
                    mmt = P.emit("pe", lambda h, t=t, c=c, buf=buf:
                                 h.matmul(ps_kv[:, 0:64],
                                          B.xn_bf[:, c * T + t * 128:c * T + (t + 1) * 128],
                                          wv[buf][:, c * 64:(c + 1) * 64],
                                          start=(c == 0), stop=(c == NC - 1)),
                                 waits=[v_ev[-1] if v_ev else bo_tok], inc="pe")
                v_ev.append(P.emit("act", lambda h, t=t: h.activation(
                    v_tm[:, t * 64:(t + 1) * 64], ps_kv[:, 0:64], AF.Copy), waits=[mmt], inc="act"))
            wfree[buf] = v_ev[-1]

            def elu_emit(ps_ap, cols, dst_ap, wtok):
                tmin = P.emit("dve", lambda h: h.tensor_scalar_min(fscr[:, 0:cols], ps_ap, 0.0),
                              waits=[wtok], inc="dve")
                texp = P.emit("act", lambda h: h.activation(fscr[:, 0:cols], fscr[:, 0:cols], AF.Exp),
                              waits=[tmin], inc="act")
                return P.emit("dve", lambda h: h.scalar_tensor_tensor(
                    dst_ap, ps_ap, 0.0, fscr[:, 0:cols], ALU.max, ALU.add),
                    waits=[texp], inc="dve")

            qp_toks = []
            for fh in range(2):
                for jh in range(2):
                    ps = ps1 if (fh * 2 + jh) % 2 == 0 else ps2
                    mmt = None
                    for j2 in range(2):
                        mmt = P.emit("pe", lambda h, fh=fh, jh=jh, j2=j2, ps=ps, buf=buf:
                                     h.matmul(ps[:, j2 * 512:(j2 + 1) * 512],
                                              wpj[buf][:, fh * 128:(fh + 1) * 128],
                                              q_fm[:, jh * 1024 + j2 * 512:jh * 1024 + (j2 + 1) * 512],
                                              start=True, stop=True),
                                     waits=[qk_ev[1], qp_toks[-2] if len(qp_toks) >= 2 else None],
                                     inc="pe")
                    qp_toks.append(elu_emit(ps[:, :], 1024,
                                            qp_fm[:, fh * T + jh * 1024:fh * T + (jh + 1) * 1024], mmt))
            kp_toks = []
            for t in range(NT):
                ps = ps1 if t % 2 == 0 else ps2
                mmt = P.emit("pe", lambda h, t=t, ps=ps, buf=buf:
                             h.matmul(ps[:, 0:256],
                                      k_fm[:, t * 128:(t + 1) * 128],
                                      wpj[buf][:, :], start=True, stop=True),
                             waits=[qk_ev[3], kp_toks[t - 2] if t >= 2 else qp_toks[2 + (t % 2)]],
                             inc="pe")
                kp_toks.append(elu_emit(ps[:, 0:256], 256, kp_tm[:, t * 256:(t + 1) * 256], mmt))
            kv_ev = []
            for fh in range(2):
                mm_kv = None
                mm_ks = None
                for t in range(NT):
                    w = [kp_toks[t]]
                    if t == 0:
                        w += [v_ev[-1], kv_ev[-1] if kv_ev else None]
                    mm_kv = P.emit("pe", lambda h, fh=fh, t=t:
                                   h.matmul(ps_kv[:, 64:128],
                                            kp_tm[:, t * 256 + fh * 128:t * 256 + (fh + 1) * 128],
                                            v_tm[:, t * 64:(t + 1) * 64],
                                            start=(t == 0), stop=(t == NT - 1)),
                                   waits=w, inc="pe")
                    mm_ks = P.emit("pe", lambda h, fh=fh, t=t:
                                   h.matmul(ps_ks[:, 0:1],
                                            kp_tm[:, t * 256 + fh * 128:t * 256 + (fh + 1) * 128],
                                            B.ones_cbf[:, :],
                                            start=(t == 0), stop=(t == NT - 1)),
                                   inc="pe")
                kv_ev.append(P.emit("act", lambda h, fh=fh: h.activation(
                    kv_bf[:, fh * 64:(fh + 1) * 64], ps_kv[:, 64:128], AF.Copy),
                    waits=[mm_kv], inc="act"))
                kv_ev.append(P.emit("act", lambda h, fh=fh: h.activation(
                    ksum_bf[:, fh:fh + 1], ps_ks[:, 0:1], AF.Copy), waits=[mm_ks], inc="act"))
            att_ev = []
            den_ev = []
            for jh in range(2):
                ps = ps1 if jh == 0 else ps2
                for j2 in range(2):
                    mmt = None
                    mmd = None
                    for fh in range(2):
                        w = kv_ev + [qp_toks[2 * fh + jh]]
                        mmt = P.emit("pe", lambda h, fh=fh, jh=jh, j2=j2, ps=ps:
                                     h.matmul(ps[0:64, j2 * 512:(j2 + 1) * 512],
                                              kv_bf[:, fh * 64:(fh + 1) * 64],
                                              qp_fm[:, fh * T + jh * 1024 + j2 * 512:
                                                    fh * T + jh * 1024 + (j2 + 1) * 512],
                                              start=(fh == 0), stop=(fh == 1)),
                                     waits=w, inc="pe")
                        mmd = P.emit("pe", lambda h, fh=fh, jh=jh, j2=j2:
                                     h.matmul(ps_dn[:, :],
                                              ksum_bf[:, fh:fh + 1],
                                              qp_fm[:, fh * T + jh * 1024 + j2 * 512:
                                                    fh * T + jh * 1024 + (j2 + 1) * 512],
                                              start=(fh == 0), stop=(fh == 1)),
                                     waits=[den_ev[-1] if den_ev else None], inc="pe")
                    den_ev.append(P.emit("act", lambda h, jh=jh, j2=j2: h.activation(
                        zrow[:, jh * 1024 + j2 * 512:jh * 1024 + (j2 + 1) * 512],
                        ps_dn[:, :], AF.Copy), waits=[mmd], inc="act"))
                att_ev.append(P.emit("act", lambda h, jh=jh, ps=ps: h.activation(
                    fscr[0:64, jh * 1024:(jh + 1) * 1024], ps[0:64, :], AF.Copy),
                    waits=[mmt], inc="act"))
            tz = P.emit("dve", lambda h: h.tensor_scalar_add(zrow[:, :], zrow[:, :], 1e-8),
                        waits=den_ev, inc="dve")
            tz = P.emit("dve", lambda h: h.reciprocal(zrow[:, :], zrow[:, :]), inc="dve")
            c_out = (h_ // 2) * T
            p0 = (h_ % 2) * 64
            zb_done = []
            for jh in range(2):
                ps = ps1 if jh == 0 else ps2
                mmz = None
                for j2 in range(2):
                    mmz = P.emit("pe", lambda h, jh=jh, j2=j2, ps=ps: h.matmul(
                        ps[0:64, j2 * 512:(j2 + 1) * 512],
                        B.ones_r32[:, 0:64],
                        zrow[:, jh * 1024 + j2 * 512:jh * 1024 + (j2 + 1) * 512],
                        start=True, stop=True),
                        waits=[tz, att_ev[jh]], inc="pe")
                zb_done.append(P.emit("dve", lambda h, jh=jh, ps=ps, c_out=c_out, p0=p0:
                                      h.tensor_tensor(
                                          att_all[p0:p0 + 64, c_out + jh * 1024:c_out + (jh + 1) * 1024],
                                          fscr[0:64, jh * 1024:(jh + 1) * 1024],
                                          ps[0:64, :], ALU.mult),
                                      waits=[mmz], inc="dve"))
            prev_att_done = zb_done[-1]
            P.flush()
        wo_free = {0: None, 1: None}

        def load_wo(c, o, buf):
            return P.dma("pool", lambda h, c=c, o=o, buf=buf: h.dma_start(
                wo[buf][:, :], w_out[c * 128:(c + 1) * 128, o * 128:(o + 1) * 128]),
                "gB" if buf == 0 else "gC", waits=[wo_free[buf]])

        seq = [(o, c) for o in range(NC) for c in range(NC)]
        pftok = load_wo(seq[0][1], seq[0][0], 0)
        ev_prev = {}
        for idx, (o, c) in enumerate(seq):
            buf = idx % 2
            cur = pftok
            if idx + 1 < len(seq):
                o2, c2 = seq[idx + 1]
                pftok = load_wo(c2, o2, (idx + 1) % 2)
            mmt = {}
            for jh in range(2):
                ps = ps1 if jh == 0 else ps2
                m = None
                for j2 in range(2):
                    m = P.emit("pe", lambda h, o=o, c=c, jh=jh, j2=j2, ps=ps, buf=buf:
                               h.matmul(ps[:, j2 * 512:(j2 + 1) * 512],
                                        wo[buf][:, :],
                                        att_all[:, c * T + jh * 1024 + j2 * 512:
                                                c * T + jh * 1024 + (j2 + 1) * 512],
                                        start=(c == 0), stop=(c == NC - 1)),
                               waits=[cur, prev_att_done,
                                      ev_prev.get((o - 1, jh)) if c == 0 else None],
                               inc="pe")
                mmt[jh] = m
            wo_free[buf] = mmt[1]
            if c == NC - 1:
                for jh in range(2):
                    ps = ps1 if jh == 0 else ps2
                    ev_prev[(o, jh)] = P.emit(
                        "dve", lambda h, o=o, jh=jh, ps=ps: h.scalar_tensor_tensor(
                            B.x_fm[:, o * T + jh * 1024:o * T + (jh + 1) * 1024],
                            ps[:, :], bout[:, o:o + 1],
                            B.x_fm[:, o * T + jh * 1024:o * T + (jh + 1) * 1024],
                            ALU.add, ALU.add),
                        waits=[mmt[jh], bo_tok], inc="dve")
        P.barrier()
        P.flush()


def emit_l0_ffn(B):
    P, nc = B.P, B.nc
    NTOK = 512
    NTILES = T // NTOK
    w1T = B.prep_aps["L0_ffn_w1_w_T"]; w1tok = B.prep_toks["L0_ffn_w1_w_T"]
    w2T = B.prep_aps["L0_ffn_w2_w_T"]; w2tok = B.prep_toks["L0_ffn_w2_w_T"]
    h1T = B.prep_aps["L0_ffn_h1_w_T"]; h1tok = B.prep_toks["L0_ffn_h1_w_T"]
    h2T = B.prep_aps["L0_ffn_h2_w_T"]; h2tok = B.prep_toks["L0_ffn_h2_w_T"]
    w3T = B.prep_aps["L0_ffn_w3_w_T"]; w3tok = B.prep_toks["L0_ffn_w3_w_T"]
    with contextlib.ExitStack() as st:
        def sb(name, shape, dtype=BF16):
            return st.enter_context(nc.sbuf_tensor(name, list(shape), dtype))
        a_bf = sb("l0f_a", (128, 32 * NTOK))
        b_bf = sb("l0f_b", (128, 32 * NTOK))
        wtile = [sb(f"l0f_w{i}", (128, NC * 128)) for i in range(4)]
        htile = [sb(f"l0f_h{i}", (128, 32 * 128)) for i in range(2)]
        scols = sb("l0f_scols", (128, 5 * 32), F32)
        silu_t = sb("l0f_silu", (128, NTOK))
        ps = [st.enter_context(nc.psum_tensor(f"l0fps{i}", [128, NTOK], F32)) for i in range(4)]

        snat = sb("l0f_snat", (32, 128), F32)
        sc_toks = []
        with nc.psum_tensor("l0f_sps", [128, 32], F32) as sps:
            for i, nm in enumerate(("L0_ffn_w1_s", "L0_ffn_w2_s", "L0_ffn_h1_s", "L0_ffn_h2_s", "L0_ffn_w3_s")):
                n = IN_SHAPES[nm][0]
                nr = n // 128
                dt = P.dma("pool", lambda h, nm=nm, nr=nr: h.dma_start(
                    snat[0:nr, :], B.ins[nm].rearrange("(c p) -> c p", p=128)),
                    "gA", waits=[sc_toks[-1] if sc_toks else None])
                tt = P.emit("pe", lambda h, nr=nr: h.transpose(
                    sps[:, 0:nr], snat[0:nr, :], B.ident32[0:nr, 0:nr]),
                    waits=[dt, sc_toks[-1] if sc_toks else None, B.consts_tok], inc="pe")
                sc_toks.append(P.emit("act", lambda h, i=i, nr=nr: h.activation(
                    scols[:, i * 32:i * 32 + nr], sps[:, 0:nr], AF.Copy),
                    waits=[tt], inc="act"))
            P.flush()
        s1c = lambda oc: scols[:, 0 * 32 + oc:0 * 32 + oc + 1]
        s2c = lambda oc: scols[:, 1 * 32 + oc:1 * 32 + oc + 1]
        sh1c = lambda oc: scols[:, 2 * 32 + oc:2 * 32 + oc + 1]
        sh2c = lambda oc: scols[:, 3 * 32 + oc:3 * 32 + oc + 1]
        s3c = lambda oc: scols[:, 4 * 32 + oc:4 * 32 + oc + 1]

        wfree = {0: None, 1: None, 2: None, 3: None}
        hfree = {0: None, 1: None}
        wtok_of = {id(w1T): w1tok, id(w2T): w2tok, id(h1T): h1tok, id(h2T): h2tok, id(w3T): w3tok}
        allprep = w1tok + w2tok + h1tok + h2tok + w3tok
        for _e in ("sp", "pool", "dve", "act"):
            P.emit(_e, lambda h: h.nop(), waits=allprep)

        def load_w(srcT, oc, buf, Ichunks):
            ch = ("fw" if Ichunks == 8 else "fh") + "ABCD"[buf]
            return P.dma("sp", lambda h, srcT=srcT, oc=oc, buf=buf, Ichunks=Ichunks: h.dma_start(
                (wtile if Ichunks == 8 else htile)[buf][:, 0:Ichunks * 128],
                srcT[:, oc * Ichunks * 128:(oc + 1) * Ichunks * 128]),
                ch, waits=[(wfree if Ichunks == 8 else hfree)[buf]] + wtok_of[id(srcT)])

        P.barrier()
        for nt in range(NTILES):
            ncol = (nt * NTOK, (nt + 1) * NTOK)
            pf1 = load_w(w1T, 0, 0, 8)
            pf2 = load_w(w2T, 0, 2, 8)
            a_toks = []
            for oc in range(32):
                b1 = oc % 2
                b2 = 2 + oc % 2
                cur1, cur2 = pf1, pf2
                if oc < 31:
                    pf1 = load_w(w1T, oc + 1, (oc + 1) % 2, 8)
                    pf2 = load_w(w2T, oc + 1, 2 + (oc + 1) % 2, 8)
                psA = ps[oc % 2]
                psB = ps[2 + oc % 2]
                m1 = None
                m2 = None
                for c in range(NC):
                    m1 = P.emit("pe", lambda h, oc=oc, c=c, psA=psA, b1=b1:
                                h.matmul(psA[:, :], wtile[b1][:, c * 128:(c + 1) * 128],
                                         B.xn_bf[:, c * T + ncol[0]:c * T + ncol[1]],
                                         start=(c == 0), stop=(c == NC - 1)),
                                waits=[cur1, a_toks[-2] if len(a_toks) >= 2 else None], inc="pe")
                for c in range(NC):
                    m2 = P.emit("pe", lambda h, oc=oc, c=c, psB=psB, b2=b2:
                                h.matmul(psB[:, :], wtile[b2][:, c * 128:(c + 1) * 128],
                                         B.xn_bf[:, c * T + ncol[0]:c * T + ncol[1]],
                                         start=(c == 0), stop=(c == NC - 1)),
                                waits=[cur2], inc="pe")
                wfree[b1] = m1
                wfree[b2] = m2
                tsil = P.emit("act", lambda h, oc=oc, psA=psA: h.activation(
                    silu_t[:, :], psA[:, :], AF.Silu, scale=s1c(oc)),
                    waits=[m1, a_toks[-1] if a_toks else None] + sc_toks, inc="act")
                a_toks.append(P.emit("dve", lambda h, oc=oc, psB=psB: h.scalar_tensor_tensor(
                    a_bf[:, oc * NTOK:(oc + 1) * NTOK], psB[:, :], s2c(oc), silu_t[:, :],
                    ALU.mult, ALU.mult),
                    waits=[m2, tsil], inc="dve"))
            def big_layer(srcT, stok, src_buf, dst_buf, scol_fn, dst_toks_prev):
                pfh = load_w(srcT, 0, 0, 32)
                toks = []
                for oc in range(32):
                    cur = pfh
                    if oc < 31:
                        pfh = load_w(srcT, oc + 1, (oc + 1) % 2, 32)
                    psA = ps[oc % 2]
                    mm = None
                    for c2 in range(32):
                        mm = P.emit("pe", lambda h, oc=oc, c2=c2, psA=psA:
                                    h.matmul(psA[:, :], htile[oc % 2][:, c2 * 128:(c2 + 1) * 128],
                                             src_buf[:, c2 * NTOK:(c2 + 1) * NTOK],
                                             start=(c2 == 0), stop=(c2 == 31)),
                                    waits=[cur, dst_toks_prev(oc),
                                           toks[-2] if len(toks) >= 2 else None], inc="pe")
                    hfree[oc % 2] = mm
                    toks.append(P.emit("act", lambda h, oc=oc, psA=psA: h.activation(
                        dst_buf[:, oc * NTOK:(oc + 1) * NTOK], psA[:, :], AF.Gelu, scale=scol_fn(oc)),
                        waits=[mm], inc="act"))
                return toks
            b_toks = big_layer(h1T, h1tok, a_bf, b_bf, sh1c, lambda oc: a_toks[-1])
            c_toks = big_layer(h2T, h2tok, b_bf, a_bf, sh2c, lambda oc: b_toks[-1])
            pfo = load_w(w3T, 0, 0, 32)
            o_toks = []
            for o in range(NC):
                cur = pfo
                if o < NC - 1:
                    pfo = load_w(w3T, o + 1, (o + 1) % 2, 32)
                psA = ps[o % 2]
                mm = None
                for c2 in range(32):
                    mm = P.emit("pe", lambda h, o=o, c2=c2, psA=psA:
                                h.matmul(psA[:, :], htile[o % 2][:, c2 * 128:(c2 + 1) * 128],
                                         a_bf[:, c2 * NTOK:(c2 + 1) * NTOK],
                                         start=(c2 == 0), stop=(c2 == 31)),
                                waits=[cur, c_toks[-1], o_toks[-2] if len(o_toks) >= 2 else None],
                                inc="pe")
                hfree[o % 2] = mm
                o_toks.append(P.emit("dve", lambda h, o=o, psA=psA: h.scalar_tensor_tensor(
                    B.x_fm[:, o * T + ncol[0]:o * T + ncol[1]], psA[:, :], s3c(o),
                    B.x_fm[:, o * T + ncol[0]:o * T + ncol[1]],
                    ALU.mult, ALU.add),
                    waits=[mm], inc="dve"))
            P.barrier(("pe", "act", "dve"))
            P.flush()
        P.barrier()
        P.flush()


def gate_logits_probs(B, gate_name, probs_tm, ps_g, gT, grow_neg, deps=()):
    P, nc = B.P, B.nc
    toks = []
    for t in range(NT):
        mmt = None
        for c in range(NC):
            mmt = P.emit("pe", lambda h, t=t, c=c:
                         h.matmul(ps_g[:, (t % 2) * 512:(t % 2) * 512 + 8],
                                  B.x_fm[:, c * T + t * 128:c * T + (t + 1) * 128],
                                  gT[:, c * 8:(c + 1) * 8],
                                  start=(c == 0), stop=False),
                         waits=list(deps) + [toks[t - 2] if t >= 2 else None], inc="pe")
        mmt = P.emit("pe", lambda h, t=t:
                     h.matmul(ps_g[:, (t % 2) * 512:(t % 2) * 512 + 8],
                              B.murow[:, t * 128:(t + 1) * 128],
                              grow_neg[:, :], start=False, stop=True),
                     inc="pe")
        tl = P.emit("act", lambda h, t=t: h.activation(
            probs_tm[:, t * 8:(t + 1) * 8], ps_g[:, (t % 2) * 512:(t % 2) * 512 + 8],
            AF.Copy, scale=B.rs_t[:, t:t + 1]), waits=[mmt], inc="act")
        nm = P.emit("dve", lambda h, t=t: h.reduce_max(
            B.scr81[:, 0:1], probs_tm[:, t * 8:(t + 1) * 8], AX.X, negate=True),
            waits=[tl], inc="dve")
        te = P.emit("act", lambda h, t=t: h.activation(
            probs_tm[:, t * 8:(t + 1) * 8], probs_tm[:, t * 8:(t + 1) * 8], AF.Exp,
            bias=B.scr81[:, 0:1], accum_out=B.scr81[:, 1:2]), waits=[nm], inc="act")
        tr = P.emit("dve", lambda h: h.reciprocal(B.scr81[:, 2:3], B.scr81[:, 1:2]),
                    waits=[te], inc="dve")
        toks.append(P.emit("act", lambda h, t=t: h.activation(
            probs_tm[:, t * 8:(t + 1) * 8], probs_tm[:, t * 8:(t + 1) * 8], AF.Copy,
            scale=B.scr81[:, 2:3]), waits=[tr], inc="act"))
    return toks[-1]





def load_gateT(B, gate_name, gT, grow_neg, ps_g, transposed=False):
    P, nc = B.P, B.nc
    g_ap = B.ins[gate_name]
    with nc.sbuf_tensor("gnat", [8, 1024], F32) as gnat:
        if not transposed:
            dt = P.dma("pool", lambda h: h.dma_start(gnat[:, :], g_ap), "gA")
            tt = None
            for c in range(NC):
                tt = P.emit("pe", lambda h, c=c: h.transpose(
                    ps_g[:, c * 8:(c + 1) * 8], gnat[0:8, c * 128:(c + 1) * 128],
                    B.ident32[0:8, 0:8]), waits=[dt, B.consts_tok], inc="pe")
            ev = P.emit("act", lambda h: h.activation(gT[:, 0:NC * 8], ps_g[:, 0:NC * 8],
                                                      AF.Copy), waits=[tt], inc="act")
        else:
            ev = P.dma("pool", lambda h: h.dma_start(
                gT[:, :].rearrange("p (c e) -> p c e", e=8),
                g_ap.rearrange("(c p) e -> p c e", p=128)), "gA")
        mm = None
        for c in range(NC):
            mm = P.emit("pe", lambda h, c=c: h.matmul(
                ps_g[0:1, 512:520], B.ones_c32[:, :],
                gT[:, c * 8:(c + 1) * 8], start=(c == 0), stop=(c == NC - 1)),
                waits=[ev], inc="pe")
        gv = P.emit("dve", lambda h: h.tensor_scalar_mul(grow_neg[:, :],
                                                         ps_g[0:1, 512:520], -1.0),
                    waits=[mm], inc="dve")
        P.flush()
    return ev, gv


def emit_l1(B):
    P, nc = B.P, B.nc
    with contextlib.ExitStack() as rst:
        bounce = [rst.enter_context(nc.sbuf_tensor(f"l1bounce{i}", [128, 1024], BF16))
                  for i in range(2)]
        attn_prep = []
        for nm in ("L1_attn_W_K", "L1_attn_W_Q", "L1_attn_W_V", "L1_attn_W_O"):
            _, tks = B.prep_transposed_dram(nm, bounce)
            attn_prep += tks
        for nm in ("L1_ffn_w1", "L1_ffn_w2"):
            B.prep_cast(nm)
        probs_tm = rst.enter_context(nc.sbuf_tensor("l1_probs", [128, NT * 8], F32))
        Rloc = rst.enter_context(nc.sbuf_tensor("l1_Rloc", [128, 128], F32))
        Rg = rst.enter_context(nc.sbuf_tensor("l1_Rg", [128, 1024], F32))
        B.scr81 = rst.enter_context(nc.sbuf_tensor("l1_scr81", [128, 8], F32))
        gT = rst.enter_context(nc.sbuf_tensor("l1_gT", [128, NC * 8 + 8], F32))
        grow_neg = rst.enter_context(nc.sbuf_tensor("l1_grow", [1, 8], F32))
        cin = nc.dram_tensor("l1_cc_in", [128, 128], F32).ap()
        cout = nc.dram_tensor("l1_cc_out", [1024, 128], F32).ap()
        B.layernorm(want_tok_stats=True, rows_stack=rst)
        with nc.psum_tensor("l1psg", [128, 1024], F32) as ps_g:
            ev, gv = load_gateT(B, "L1_ffn_gate_w", gT, grow_neg, ps_g)
            gtok = gate_logits_probs(B, "L1_ffn_gate_w", probs_tm, ps_g, gT, grow_neg,
                                     deps=[ev, gv])
            rtok = routing_R(B, probs_tm, Rloc, gtok)
            P.flush()
        ctk = P.dma("pool", lambda h: h.dma_start(cin, Rloc[:, :]), "l1cc", waits=[rtok])
        cctok = P.emit("pool", lambda h: h.collective_compute(
            "AllGather", ALU.bypass, replica_groups=[list(range(NCORE))],
            ins=[cin.opt()], outs=[cout.opt()]), waits=[ctk], inc="cc")
        ldR = P.dma("pool", lambda h: h.dma_start(
            Rg[:, :].rearrange("p (r f) -> p r f", f=128),
            cout.rearrange("(r p) f -> p r f", p=128)), "l1cc", waits=[cctok])
        emit_l1_attn(B)
        theta, btok = bisect_threshold(B, Rg, ldR)
        sstok = None
        for t in range(NT):
            kp = P.emit("dve", lambda h, t=t: h.tensor_tensor(
                B.scr81[:, 0:8], Rloc[:, t * 8:(t + 1) * 8], theta, ALU.is_gt),
                waits=[sstok, btok], inc="dve")
            sstok = P.emit("dve", lambda h, t=t: h.tensor_mul(
                probs_tm[:, t * 8:(t + 1) * 8], B.scr81[:, 0:8],
                probs_tm[:, t * 8:(t + 1) * 8]), inc="dve")
        P.flush()
        B._l1_bisect_stack.close()
        emit_l1_ffn(B, probs_tm, sstok, attn_prep)
        P.barrier()
        P.flush()


def routing_R(B, probs_tm, Rloc, gtok):
    P, nc = B.P, B.nc
    with contextlib.ExitStack() as st:
        pp = st.enter_context(nc.sbuf_tensor("l1_pp", [128, 16], F32))
        S = st.enter_context(nc.sbuf_tensor("l1_S", [128, 8], F32))
        rk = st.enter_context(nc.sbuf_tensor("l1_rk", [128, 8], F32))
        cmp = st.enter_context(nc.sbuf_tensor("l1_cmp", [128, 8], F32))
        cmp2 = st.enter_context(nc.sbuf_tensor("l1_cmp2", [128, 8], F32))
        last = None
        for t in range(NT):
            pt = probs_tm[:, t * 8:(t + 1) * 8]
            P.emit("dve", lambda h, pt=pt: h.tensor_copy(pp[:, 0:8], pt), waits=[gtok],
                   inc="dve")
            P.emit("dve", lambda h, pt=pt: h.tensor_copy(pp[:, 8:16], pt), inc="dve")
            for k in range(1, 8):
                P.emit("dve", lambda h, pt=pt, k=k: h.tensor_tensor(
                    cmp[:, :], pp[:, k:k + 8], pt, ALU.is_gt), inc="dve")
                P.emit("dve", lambda h, k=k: h.tensor_mul(cmp2[:, :], cmp[:, :], pp[:, k:k + 8]),
                       inc="dve")
                if k == 1:
                    P.emit("dve", lambda h: h.tensor_copy(S[:, :], cmp2[:, :]), inc="dve")
                    P.emit("dve", lambda h: h.tensor_copy(rk[:, :], cmp[:, :]), inc="dve")
                else:
                    P.emit("dve", lambda h: h.tensor_add(S[:, :], S[:, :], cmp2[:, :]), inc="dve")
                    P.emit("dve", lambda h: h.tensor_add(rk[:, :], rk[:, :], cmp[:, :]), inc="dve")
            P.emit("dve", lambda h: h.tensor_single_scalar(cmp[:, :], S[:, :], 0.9, ALU.is_lt),
                   inc="dve")
            P.emit("dve", lambda h, pt=pt: h.tensor_sub(S[:, :], pt, rk[:, :]), inc="dve")
            P.emit("dve", lambda h: h.tensor_scalar_add(S[:, :], S[:, :], -1.0), inc="dve")
            P.emit("dve", lambda h: h.scalar_tensor_tensor(
                S[:, :], cmp[:, :], 100.0, S[:, :], ALU.mult, ALU.add), inc="dve")
            last = P.emit("dve", lambda h, t=t: h.tensor_scalar_add(
                Rloc[:, t * 8:(t + 1) * 8], S[:, :], -100.0), inc="dve")
        P.flush()
    return last


def bisect_threshold(B, Rg, ldR):
    P, nc = B.P, B.nc
    st = contextlib.ExitStack()
    lo2 = st.enter_context(nc.sbuf_tensor("l1_lo", [128, 16], F32))
    hi2 = st.enter_context(nc.sbuf_tensor("l1_hi", [128, 16], F32))
    mid = st.enter_context(nc.sbuf_tensor("l1_mid", [128, 8], F32))
    gt_all = st.enter_context(nc.sbuf_tensor("l1_gtall", [128, 1024], F32))
    crow = st.enter_context(nc.sbuf_tensor("l1_crow", [1, 16], F32))
    cb = st.enter_context(nc.sbuf_tensor("l1_cb", [128, 8], F32))
    tbl = st.enter_context(nc.sbuf_tensor("l1_tbl", [128, 8], F32))
    B._l1_bisect_stack = st
    t0 = P.emit("dve", lambda h: h.memset(lo2[:, 0:8], -10.0), inc="dve")
    t0 = P.emit("dve", lambda h: h.memset(hi2[:, 0:8], 0.0), inc="dve")
    t0 = P.emit("dve", lambda h: h.memset(mid[:, :], -5.0), inc="dve")
    with nc.psum_tensor("l1psb", [1, 1024], F32) as psb,          nc.psum_tensor("l1psc", [128, 8], F32) as psc:
        for it in range(36):
            gts = []
            Rg3 = Rg[:, :].rearrange("p (f e) -> p f e", e=8)
            gt3 = gt_all[:, :].rearrange("p (f e) -> p f e", e=8)
            for e in range(8):
                gts.append(P.emit("dve", lambda h, e=e: h.tensor_scalar(
                    gt3[:, :, e], Rg3[:, :, e], mid[:, e:e + 1], None, ALU.is_gt),
                    waits=[ldR], inc="dve"))
            mm = None
            for j in range(2):
                mm = P.emit("pe", lambda h, j=j: h.matmul(
                    psb[:, j * 512:(j + 1) * 512], B.ones_c32[:, :],
                    gt_all[:, j * 512:(j + 1) * 512], start=True, stop=True),
                    waits=[gts[-1]], inc="pe")
            rs = P.emit("dve", lambda h: h.tensor_reduce(
                crow[:, 0:8], psb[0:1, :].rearrange("p (f e) -> p f e", e=8).transpose([0, 2, 1]),
                AX.X, ALU.add), waits=[mm], inc="dve")
            bc = P.emit("pe", lambda h: h.matmul(psc[:, :], B.ones_r32[:, :], crow[:, 0:8],
                                                 start=True, stop=True), waits=[rs], inc="pe")
            ev = P.emit("act", lambda h: h.activation(cb[:, :], psc[:, :], AF.Copy),
                        waits=[bc], inc="act")
            cm = P.emit("dve", lambda h: h.tensor_single_scalar(cb[:, :], cb[:, :], 2048.0,
                                                                ALU.is_gt), waits=[ev], inc="dve")
            a = (it % 2) * 8
            b = ((it + 1) % 2) * 8
            P.emit("dve", lambda h, a=a: h.tensor_sub(tbl[:, :], mid[:, :], lo2[:, a:a + 8]),
                   inc="dve")
            P.emit("dve", lambda h: h.tensor_mul(tbl[:, :], tbl[:, :], cb[:, :]), inc="dve")
            P.emit("dve", lambda h, a=a, b=b: h.tensor_add(lo2[:, b:b + 8], lo2[:, a:a + 8],
                                                           tbl[:, :]), inc="dve")
            P.emit("dve", lambda h, a=a: h.tensor_sub(tbl[:, :], hi2[:, a:a + 8], mid[:, :]),
                   inc="dve")
            P.emit("dve", lambda h: h.tensor_mul(tbl[:, :], tbl[:, :], cb[:, :]), inc="dve")
            P.emit("dve", lambda h, b=b: h.tensor_add(hi2[:, b:b + 8], mid[:, :], tbl[:, :]),
                   inc="dve")
            P.emit("dve", lambda h, b=b: h.tensor_add(mid[:, :], lo2[:, b:b + 8],
                                                      hi2[:, b:b + 8]), inc="dve")
            btok = P.emit("dve", lambda h: h.tensor_scalar_mul(mid[:, :], mid[:, :], 0.5),
                          inc="dve")
        P.flush()
    return hi2[:, (36 % 2) * 8:(36 % 2) * 8 + 8], btok


def emit_l1_attn(B):
    P, nc = B.P, B.nc
    with contextlib.ExitStack() as st:
        def sb(name, shape, dtype=BF16):
            return st.enter_context(nc.sbuf_tensor(name, list(shape), dtype))
        wkb = [sb(f"l1_wkb{i}", (128, 1024)) for i in range(2)]
        wqb = [sb(f"l1_wqb{i}", (128, 1024)) for i in range(2)]
        wvb = [sb(f"l1_wvb{i}", (128, 1024)) for i in range(2)]
        wob = [sb(f"l1_wob{i}", (128, 1024)) for i in range(2)]
        k_bf = sb("l1_kbf", (64, T))
        q_bf = sb("l1_qbf", (64, T))
        v_tm = sb("l1_vtm", (128, NT * 64))
        vs_bf = sb("l1_vs", (128, 64))
        att_all = sb("l1_attall", (128, NC * T))
        A_bf = sb("l1_A", (128, T))
        kn_t = sb("l1_knt", (128, NT + 4), F32)
        fscr = sb("l1_fscr", (128, T), F32)
        rowsc = sb("l1_rowsc", (1, T), F32)
        mneg = sb("l1_mneg", (128, 128), F32)
        z512 = sb("l1_z512", (128, 512))
        ps_s = st.enter_context(nc.psum_tensor("l1pss", [128, 1024], F32))
        ps_z = st.enter_context(nc.psum_tensor("l1psz", [64, T], F32))
        ps_sm = st.enter_context(nc.psum_tensor("l1pssm", [128, 512], F32))
        ps_row = st.enter_context(nc.psum_tensor("l1psrow", [1, 512], F32))

        ztok = P.emit("pool", lambda h: h.memset(z512[:, :], 0.0), inc="pool")
        mtok = P.emit("pool", lambda h: h.memset(mneg[:, :], 0.0), inc="pool")
        mtok = P.emit("pool", lambda h: h.affine_select(
            mneg[:, :], mneg[:, :], pattern=[[-1, 128]], compare_op=ALU.is_ge,
            fill=-1e30, base=0, channel_multiplier=1), waits=[mtok], inc="pool")

        kTS = B.prep_aps["L1_attn_W_K_T"]; ktk = B.prep_toks["L1_attn_W_K_T"]
        qTS = B.prep_aps["L1_attn_W_Q_T"]; qtk = B.prep_toks["L1_attn_W_Q_T"]
        vTS = B.prep_aps["L1_attn_W_V_T"]; vtk = B.prep_toks["L1_attn_W_V_T"]
        oTS = B.prep_aps["L1_attn_W_O_T"]; otk = B.prep_toks["L1_attn_W_O_T"]
        wfree = {}

        def load_hp(srcT, dst, ch, hp, bi, ready):
            return P.dma("sp", lambda h, srcT=srcT, dst=dst, hp=hp, bi=bi: h.dma_start(
                dst[bi][:, :], srcT[:, hp * 1024:(hp + 1) * 1024]),
                ch + str(bi), waits=[wfree.get((ch, bi))] + ready)

        def load_pair(hp, bi):
            return [load_hp(kTS, wkb, "l1k", hp, bi, ktk),
                    load_hp(qTS, wqb, "l1q", hp, bi, qtk),
                    load_hp(vTS, wvb, "l1v", hp, bi, vtk)]

        pf = load_pair(0, 0)
        prev_tail = None
        for h_ in range(16):
            hp, hh = h_ // 2, h_ % 2
            if hh == 0:
                cur = pf
                if hp < 7:
                    pf = load_pair(hp + 1, (hp + 1) % 2)
            bi = hp % 2
            kq_ev = []
            for (Wt, dst, scale) in ((wkb[bi], k_bf, 0.125), (wqb[bi], q_bf, 1.0)):
                for jh in range(2):
                    mmt = None
                    for ec in range(NC):
                        for j2 in range(2):
                            w = cur + [prev_tail]
                            if ec == 0 and kq_ev:
                                w.append(kq_ev[-1])
                            mmt = P.emit("pe", lambda h, Wt=Wt, ec=ec, jh=jh, j2=j2:
                                         h.matmul(ps_s[0:64, j2 * 512:(j2 + 1) * 512],
                                                  Wt[:, ec * 128 + hh * 64:ec * 128 + hh * 64 + 64],
                                                  B.xn_bf[:, ec * T + jh * 1024 + j2 * 512:
                                                          ec * T + jh * 1024 + (j2 + 1) * 512],
                                                  start=(ec == 0), stop=(ec == NC - 1)),
                                         waits=w, inc="pe")
                    kq_ev.append(P.emit("act", lambda h, dst=dst, jh=jh, scale=scale:
                                        h.activation(dst[:, jh * 1024:(jh + 1) * 1024],
                                                     ps_s[0:64, :], AF.Copy, scale=scale),
                                        waits=[mmt], inc="act"))
            v_ev = []
            for t in range(NT):
                mmt = None
                for ec in range(NC):
                    mmt = P.emit("pe", lambda h, t=t, ec=ec:
                                 h.matmul(ps_sm[:, 0:64],
                                          B.xn_bf[:, ec * T + t * 128:ec * T + (t + 1) * 128],
                                          wvb[bi][:, ec * 128 + hh * 64:ec * 128 + hh * 64 + 64],
                                          start=(ec == 0), stop=(ec == NC - 1)),
                                 waits=[v_ev[-1] if v_ev else (prev_tail or kq_ev[-1])], inc="pe")
                v_ev.append(P.emit("act", lambda h, t=t: h.activation(
                    v_tm[:, t * 64:(t + 1) * 64], ps_sm[:, 0:64], AF.Copy),
                    waits=[mmt], inc="act"))
            sq1 = P.emit("act", lambda h: h.activation(fscr[0:64, :], k_bf[:, :], AF.Square),
                         waits=[kq_ev[1], v_ev[-1]], inc="act")
            kn_ev = []
            for j in range(4):
                mmn = P.emit("pe", lambda h, j=j: h.matmul(
                    ps_row[:, :], B.ones_c32[0:64, :], fscr[0:64, j * 512:(j + 1) * 512],
                    start=True, stop=True), waits=[sq1, kn_ev[-1] if kn_ev else None], inc="pe")
                kn_ev.append(P.emit("act", lambda h, j=j: h.activation(
                    rowsc[:, j * 512:(j + 1) * 512], ps_row[:, :], AF.Sqrt),
                    waits=[mmn], inc="act"))
            sq2 = P.emit("act", lambda h: h.activation(fscr[0:64, :], q_bf[:, :], AF.Square),
                         waits=[kn_ev[-1], kq_ev[3]], inc="act")
            qm_ev = []
            for j in range(4):
                mmn = P.emit("pe", lambda h, j=j: h.matmul(
                    ps_row[:, :], B.ones_c32[0:64, :], fscr[0:64, j * 512:(j + 1) * 512],
                    start=True, stop=True), waits=[sq2, qm_ev[-1] if qm_ev else kn_ev[-1]],
                    inc="pe")
                qm_ev.append(P.emit("dve", lambda h, j=j: h.reduce_max(
                    kn_t[:, NT + 1:NT + 2] if False else kn_t[0:1, NT + j:NT + j + 1],
                    ps_row[:, :], AX.X), waits=[mmn], inc="dve"))
            t1 = P.emit("dve", lambda h: h.reduce_max(kn_t[0:1, NT:NT + 1],
                                                      kn_t[0:1, NT:NT + 4], AX.X),
                        waits=qm_ev, inc="dve")
            t1 = P.emit("act", lambda h: h.activation(kn_t[0:1, NT:NT + 1],
                                                      kn_t[0:1, NT:NT + 1], AF.Sqrt),
                        waits=[t1], inc="act")
            mmb = P.emit("pe", lambda h: h.matmul(ps_sm[:, 64:65], B.ones_r32[:, :],
                                                  kn_t[0:1, NT:NT + 1], start=True, stop=True),
                         waits=[t1], inc="pe")
            nmq = P.emit("dve", lambda h: h.tensor_scalar_mul(kn_t[:, NT:NT + 1],
                                                              ps_sm[:, 64:65], -1.0),
                         waits=[mmb], inc="dve")
            kt_ev = []
            for t in range(NT):
                tt = P.emit("pe", lambda h, t=t: h.transpose(
                    ps_sm[:, 128 + (t % 2):128 + (t % 2) + 1],
                    rowsc[:, t * 128:(t + 1) * 128], B.ident32[0:1, 0:1]),
                    waits=[kn_ev[-1], nmq, kt_ev[-1] if kt_ev else None], inc="pe")
                kt_ev.append(P.emit("act", lambda h, t=t: h.activation(
                    kn_t[:, t:t + 1], ps_sm[:, 128 + (t % 2):128 + (t % 2) + 1], AF.Copy),
                    waits=[tt], inc="act"))
            zinit = None
            for cc4 in range(4):
                zinit = P.emit("pe", lambda h, cc4=cc4: h.matmul(
                    ps_z[:, cc4 * 512:(cc4 + 1) * 512], B.identbf[:, 0:64], z512[:, :],
                    start=True, stop=False),
                    waits=[ztok, prev_tail, B.consts_tok], inc="pe")
            prev_exp = None
            prev_den = zinit
            for ct in range(NT):
                C_len = (ct + 1) * 128
                nb = P.emit("dve", lambda h, ct=ct: h.tensor_mul(
                    kn_t[:, NT + 2:NT + 3], kn_t[:, ct:ct + 1], kn_t[:, NT:NT + 1]),
                    waits=[nmq, kt_ev[ct], prev_exp], inc="dve")
                nch = (C_len + 511) // 512
                den_parts = []
                for cc in range(nch):
                    c0 = cc * 512
                    cl = min(512, C_len - c0)
                    pcol = (cc % 2) * 512
                    mms = P.emit("pe", lambda h, ct=ct, c0=c0, cl=cl, pcol=pcol:
                                 h.matmul(ps_s[:, pcol:pcol + cl],
                                          k_bf[:, ct * 128:(ct + 1) * 128],
                                          q_bf[:, c0:c0 + cl],
                                          start=True, stop=True),
                                 waits=[prev_exp, prev_den, kq_ev[1], kq_ev[3]], inc="pe")
                    dstart = ct * 128 - c0
                    if 0 <= dstart < cl:
                        md = P.emit("dve", lambda h, ct=ct, c0=c0, pcol=pcol, dstart=dstart:
                                    h.tensor_tensor(fscr[:, 0:128],
                                                    ps_s[:, pcol + dstart:pcol + dstart + 128],
                                                    mneg[:, :], ALU.add),
                                    waits=[mms, mtok], inc="dve")
                        pi = len(den_parts)
                        e1 = P.emit("act", lambda h, ct=ct, c0=c0, dstart=dstart, pi=pi:
                                    h.activation(A_bf[:, c0 + dstart:c0 + dstart + 128],
                                                 fscr[:, 0:128], AF.Exp,
                                                 bias=kn_t[:, NT + 2:NT + 3],
                                                 accum_out=fscr[:, 144 + pi:145 + pi]),
                                    waits=[md, nb], inc="act")
                        den_parts.append((pi, e1))
                        if dstart > 0:
                            pi = len(den_parts)
                            e2 = P.emit("act", lambda h, c0=c0, pcol=pcol, dstart=dstart, pi=pi:
                                        h.activation(A_bf[:, c0:c0 + dstart],
                                                     ps_s[:, pcol:pcol + dstart], AF.Exp,
                                                     bias=kn_t[:, NT + 2:NT + 3],
                                                     accum_out=fscr[:, 144 + pi:145 + pi]),
                                        waits=[mms, nb], inc="act")
                            den_parts.append((pi, e2))
                            prev_exp = e2
                        else:
                            prev_exp = e1
                    else:
                        pi = len(den_parts)
                        e2 = P.emit("act", lambda h, c0=c0, cl=cl, pcol=pcol, pi=pi:
                                    h.activation(A_bf[:, c0:c0 + cl],
                                                 ps_s[:, pcol:pcol + cl], AF.Exp,
                                                 bias=kn_t[:, NT + 2:NT + 3],
                                                 accum_out=fscr[:, 144 + pi:145 + pi]),
                                    waits=[mms, nb], inc="act")
                        den_parts.append((pi, e2))
                        prev_exp = e2
                acc = None
                for (pi, tk) in den_parts:
                    srcap = fscr[:, 144 + pi:145 + pi]
                    if acc is None:
                        acc = P.emit("dve", lambda h, srcap=srcap: h.tensor_copy(
                            fscr[:, 140:141], srcap), waits=[tk], inc="dve")
                    else:
                        acc = P.emit("dve", lambda h, srcap=srcap: h.tensor_add(
                            fscr[:, 140:141], fscr[:, 140:141], srcap), waits=[tk], inc="dve")
                rr = P.emit("dve", lambda h: h.reciprocal(fscr[:, 141:142], fscr[:, 140:141]),
                            waits=[acc], inc="dve")
                vs = P.emit("act", lambda h, ct=ct: h.activation(
                    vs_bf[:, :], v_tm[:, ct * 64:(ct + 1) * 64], AF.Copy,
                    scale=fscr[:, 141:142]), waits=[rr, prev_den], inc="act")
                dtoks = []
                for cc in range(nch):
                    c0 = cc * 512
                    cl = min(512, C_len - c0)
                    dtoks.append(P.emit("pe", lambda h, ct=ct, c0=c0, cl=cl:
                                        h.matmul(ps_z[:, c0:c0 + cl],
                                                 vs_bf[:, :],
                                                 A_bf[:, c0:c0 + cl],
                                                 start=False, stop=(ct == NT - 1)),
                                        waits=[vs], inc="pe"))
                prev_den = dtoks[-1]
            c_out = (h_ // 2) * T
            p0 = (h_ % 2) * 64
            ztoks = []
            for jh in range(2):
                ztoks.append(P.emit("act", lambda h, jh=jh, c_out=c_out, p0=p0: h.activation(
                    att_all[p0:p0 + 64, c_out + jh * 1024:c_out + (jh + 1) * 1024],
                    ps_z[:, jh * 1024:(jh + 1) * 1024], AF.Copy),
                    waits=[prev_den], inc="act"))
            prev_tail = ztoks[-1]
            if hh == 1:
                wfree[("l1k", bi)] = prev_tail
                wfree[("l1q", bi)] = prev_tail
                wfree[("l1v", bi)] = prev_tail
            P.flush()
        ev_prev = {}
        pfo = P.dma("sp", lambda h: h.dma_start(wob[0][:, :], oTS[:, 0:1024]),
                    "l1o0", waits=otk)
        for o in range(NC):
            bo = o % 2
            curo = pfo
            if o < NC - 1:
                pfo = P.dma("sp", lambda h, o=o: h.dma_start(
                    wob[(o + 1) % 2][:, :], oTS[:, (o + 1) * 1024:(o + 2) * 1024]),
                    "l1o" + str((o + 1) % 2),
                    waits=[wfree.get(("l1o", (o + 1) % 2))])
            mmt = {}
            for jh in range(2):
                m = None
                for c in range(NC):
                    for j2 in range(2):
                        m = P.emit("pe", lambda h, o=o, c=c, jh=jh, j2=j2, bo=bo:
                                   h.matmul(ps_s[:, j2 * 512:(j2 + 1) * 512],
                                            wob[bo][:, c * 128:(c + 1) * 128],
                                            att_all[:, c * T + jh * 1024 + j2 * 512:
                                                    c * T + jh * 1024 + (j2 + 1) * 512],
                                            start=(c == 0), stop=(c == NC - 1)),
                                   waits=[curo, prev_tail, ev_prev.get((o, jh - 1)) or
                                          ev_prev.get((o - 1, 1))], inc="pe")
                mmt[jh] = m
                ev_prev[(o, jh)] = P.emit(
                    "dve", lambda h, o=o, jh=jh: h.tensor_tensor(
                        B.x_fm[:, o * T + jh * 1024:o * T + (jh + 1) * 1024],
                        ps_s[:, :],
                        B.x_fm[:, o * T + jh * 1024:o * T + (jh + 1) * 1024], ALU.add),
                    waits=[m], inc="dve")
            wfree[("l1o", bo)] = mmt[1]
        P.barrier()
        P.flush()


def emit_l1_ffn(B, ssel_tm, sstok, attn_prep):
    P, nc = B.P, B.nc
    with contextlib.ExitStack() as st:
        def sb(name, shape, dtype=BF16):
            return st.enter_context(nc.sbuf_tensor(name, list(shape), dtype))
        w1t = sb("l1f_w1t", (128, NC * 1024))
        w2t = sb("l1f_w2t", (128, NC * 1024))
        h_bf = sb("l1f_h", (128, NC * T))
        SC = sb("l1f_SC", (128, T), F32)
        srowx = sb("l1f_srowx", (1, T), F32)
        ps = [st.enter_context(nc.psum_tensor(f"l1fps{i}", [128, 512], F32)) for i in range(4)]
        w1bf, c1tok = B.prep_cast("L1_ffn_w1")
        w2bf, c2tok = B.prep_cast("L1_ffn_w2")
        for _e in ("sp", "pool", "dve", "act"):
            P.emit(_e, lambda h: h.nop(), waits=attn_prep)
        P.barrier()
        prev_y = None
        prev_h = None
        for e in range(8):
            t1 = None
            t2 = None
            for dc in range(8):
                t1 = P.dma("sp", lambda h, e=e, dc=dc: h.dma_start(
                    w1t[:, dc * 1024:(dc + 1) * 1024],
                    w1bf[e * 1024:(e + 1) * 1024, dc * 128:(dc + 1) * 128], transpose=True),
                    "l1fw1", waits=[c1tok if e == 0 and dc == 0 else None,
                                    prev_h if dc == 0 else None])
            for fc in range(8):
                t2 = P.dma("sp", lambda h, e=e, fc=fc: h.dma_start(
                    w2t[:, fc * 1024:(fc + 1) * 1024],
                    w2bf[e * 1024:(e + 1) * 1024, fc * 128:(fc + 1) * 128], transpose=True),
                    "l1fw2", waits=[c2tok if e == 0 and fc == 0 else None,
                                    prev_y if fc == 0 else None])
            rx_ev = []
            for t in range(NT):
                mmr = P.emit("pe", lambda h, e=e, t=t: h.matmul(
                    ps[2 + t % 2][0:1, 0:128],
                    ssel_tm[:, t * 8 + e:t * 8 + e + 1], B.ident32[:, :],
                    start=True, stop=True),
                    waits=[sstok, prev_y, rx_ev[-2] if len(rx_ev) >= 2 else None], inc="pe")
                rx_ev.append(P.emit("act", lambda h, t=t: h.activation(
                    srowx[:, t * 128:(t + 1) * 128], ps[2 + t % 2][0:1, 0:128], AF.Copy),
                    waits=[mmr], inc="act"))
            sc_ev = []
            for j in range(4):
                mmb = P.emit("pe", lambda h, j=j: h.matmul(
                    ps[j % 2][:, :], B.ones_r32[:, :],
                    srowx[:, j * 512:(j + 1) * 512], start=True, stop=True),
                    waits=[rx_ev[-1], sc_ev[-2] if len(sc_ev) >= 2 else None], inc="pe")
                sc_ev.append(P.emit("act", lambda h, j=j: h.activation(
                    SC[:, j * 512:(j + 1) * 512], ps[j % 2][:, :], AF.Copy),
                    waits=[mmb], inc="act"))
            h_toks = []
            for fc in range(8):
                for jq in range(4):
                    psx = ps[(fc * 4 + jq) % 4]
                    mm = None
                    for dc in range(8):
                        mm = P.emit("pe", lambda h, fc=fc, jq=jq, dc=dc, psx=psx:
                                    h.matmul(psx[:, :],
                                             w1t[:, dc * 1024 + fc * 128:dc * 1024 + (fc + 1) * 128],
                                             B.xn_bf[:, dc * T + jq * 512:dc * T + (jq + 1) * 512],
                                             start=(dc == 0), stop=(dc == NC - 1)),
                                    waits=[t1, sc_ev[-1],
                                           h_toks[-4] if len(h_toks) >= 4 else None], inc="pe")
                    h_toks.append(P.emit("act", lambda h, fc=fc, jq=jq, psx=psx: h.activation(
                        h_bf[:, fc * T + jq * 512:fc * T + (jq + 1) * 512], psx[:, :], AF.Gelu),
                        waits=[mm], inc="act"))
            hs_toks = []
            for fc in range(8):
                hs_toks.append(P.emit("dve", lambda h, fc=fc: h.tensor_mul(
                    h_bf[:, fc * T:(fc + 1) * T], h_bf[:, fc * T:(fc + 1) * T], SC[:, :]),
                    waits=[h_toks[-1]], inc="dve"))
            prev_h = hs_toks[-1]
            y_toks = []
            for dc in range(8):
                for jq in range(4):
                    psx = ps[(dc * 4 + jq) % 4]
                    mm = None
                    for fc in range(8):
                        mm = P.emit("pe", lambda h, dc=dc, jq=jq, fc=fc, psx=psx:
                                    h.matmul(psx[:, :],
                                             w2t[:, fc * 1024 + dc * 128:fc * 1024 + (dc + 1) * 128],
                                             h_bf[:, fc * T + jq * 512:fc * T + (jq + 1) * 512],
                                             start=(fc == 0), stop=(fc == NC - 1)),
                                    waits=[t2, hs_toks[-1],
                                           y_toks[-4] if len(y_toks) >= 4 else None], inc="pe")
                    y_toks.append(P.emit("dve", lambda h, dc=dc, jq=jq, psx=psx: h.tensor_tensor(
                        B.x_fm[:, dc * T + jq * 512:dc * T + (jq + 1) * 512], psx[:, :],
                        B.x_fm[:, dc * T + jq * 512:dc * T + (jq + 1) * 512], ALU.add),
                        waits=[mm], inc="dve"))
            prev_y = y_toks[-1]
            P.flush()
        P.barrier()
        P.flush()


def emit_l2(B):
    P, nc = B.P, B.nc
    with contextlib.ExitStack() as rst:
        probs_tm = rst.enter_context(nc.sbuf_tensor("l2_probs", [128, NT * 8], F32))
        gT = rst.enter_context(nc.sbuf_tensor("l2_gT", [128, NC * 8 + 8], F32))
        grow_neg = rst.enter_context(nc.sbuf_tensor("l2_grow", [1, 8], F32))
        B.scr81 = rst.enter_context(nc.sbuf_tensor("l2_scr81", [128, 8], F32))
        B.layernorm(want_tok_stats=True, rows_stack=rst)
        with nc.psum_tensor("l2psg", [128, 1024], F32) as ps_g:
            ev, gv = load_gateT(B, "L2_ffn_gate_w", gT, grow_neg, ps_g)
            gtok = gate_logits_probs(B, "L2_ffn_gate_w", probs_tm, ps_g, gT, grow_neg,
                                     deps=[ev, gv])
            P.flush()
        emit_l2_attn(B)
        emit_l2_ffn(B, probs_tm, gtok)
        P.barrier()
        P.flush()


def emit_l2_attn(B):
    P, nc = B.P, B.nc
    with contextlib.ExitStack() as st:
        def sb(name, shape, dtype=BF16):
            return st.enter_context(nc.sbuf_tensor(name, list(shape), dtype))
        Q_fm = sb("l2_Qfm", (128, NC * T))
        Q_tm = sb("l2_Qtm", (128, NT * D))
        bq_c = sb("l2_bqc", (128, NC), F32)
        bq_row = sb("l2_bqrow", (1, D))
        negm = sb("l2_negm", (128, 2), F32)
        ps_row4 = st.enter_context(nc.psum_tensor("l2ps4", [128, 2048], F32))
        ps_t = st.enter_context(nc.psum_tensor("l2pst", [128, 2048], BF16))
        ps_o = st.enter_context(nc.psum_tensor("l2pso", [128, 512], F32))
        Wq = B.ins["L2_attn_Wq"]
        bq = B.ins["L2_attn_bq"]
        with contextlib.ExitStack() as wst:
            Wq_bf = wst.enter_context(nc.sbuf_tensor("l2_Wq", [128, NC * D], BF16))
            bq_nat = wst.enter_context(nc.sbuf_tensor("l2_bqn", [NC, 128], F32))
            wtok = P.dma("pool", lambda h: h.dma_start(
                Wq_bf[:, :].rearrange("p (c o) -> p c o", o=D),
                Wq.rearrange("(c p) o -> p c o", p=128)), "gA")
            btok = P.dma("pool", lambda h: h.dma_start(
                bq_nat[:, :], bq.rearrange("(c p) -> c p", p=128)), "gA")
            btok2 = P.dma("pool", lambda h: h.dma_start(
                bq_row[0:1, :], bq.rearrange("(o d) -> o d", o=1)), "gA")
            t1 = P.emit("pe", lambda h: h.transpose(ps_o[:, 0:NC], bq_nat[:, :],
                                                    B.ident32[0:NC, 0:NC]),
                        waits=[btok2, B.consts_tok], inc="pe")
            bc_tok = P.emit("act", lambda h: h.activation(bq_c[:, :], ps_o[:, 0:NC], AF.Copy),
                            waits=[t1], inc="act")
            qf_ev = []
            for oc in range(NC):
                for jh in range(2):
                    mmt = None
                    for ic in range(NC):
                        for j2 in range(2):
                            w = [btok2, bc_tok,
                                 qf_ev[-2] if len(qf_ev) >= 2 else None]
                            mmt = P.emit("pe", lambda h, oc=oc, jh=jh, ic=ic, j2=j2:
                                         h.matmul(ps_row4[:, (jh % 2) * 1024 + j2 * 512:
                                                          (jh % 2) * 1024 + (j2 + 1) * 512],
                                                  Wq_bf[:, ic * D + oc * 128:ic * D + oc * 128 + 128],
                                                  B.xn_bf[:, ic * T + jh * 1024 + j2 * 512:
                                                          ic * T + jh * 1024 + (j2 + 1) * 512],
                                                  start=(ic == 0), stop=(ic == NC - 1)),
                                         waits=w, inc="pe")
                    qf_ev.append(P.emit("dve", lambda h, oc=oc, jh=jh: h.tensor_scalar(
                        Q_fm[:, oc * T + jh * 1024:oc * T + (jh + 1) * 1024],
                        ps_row4[:, (jh % 2) * 1024:(jh % 2) * 1024 + 1024],
                        bq_c[:, oc:oc + 1], None, ALU.add), waits=[mmt], inc="dve"))
            qt_ev = []
            for t in range(NT):
                for oh in range(2):
                    qi = len(qt_ev) % 4
                    mmt = None
                    for ic in range(NC):
                        w = [btok2, qt_ev[-4] if len(qt_ev) >= 4 else qf_ev[-1]]
                        mmt = P.emit("pe", lambda h, t=t, oh=oh, ic=ic, qi=qi:
                                     h.matmul(ps_row4[:, qi * 512:qi * 512 + 512],
                                              B.xn_bf[:, ic * T + t * 128:ic * T + (t + 1) * 128],
                                              Wq_bf[:, ic * D + oh * 512:ic * D + (oh + 1) * 512],
                                              start=(ic == 0), stop=False),
                                     waits=w, inc="pe")
                    mmt = P.emit("pe", lambda h, t=t, oh=oh, qi=qi:
                                 h.matmul(ps_row4[:, qi * 512:qi * 512 + 512],
                                          B.ones_rbf[:, :],
                                          bq_row[0:1, oh * 512:(oh + 1) * 512],
                                          start=False, stop=True),
                                 inc="pe")
                    qt_ev.append(P.emit("act", lambda h, t=t, oh=oh, qi=qi: h.activation(
                        Q_tm[:, t * D + oh * 512:t * D + (oh + 1) * 512],
                        ps_row4[:, qi * 512:qi * 512 + 512],
                        AF.Copy), waits=[mmt], inc="act"))
            P.flush()
        A_bf = sb("l2_A", (128, T))
        At = sb("l2_At", (128, NT * 128))
        prev_exp = qt_ev[-1]
        prev_out = None
        for qt in range(NT):
            mml = None
            for kc in range(4):
                for dc in range(NC):
                    mml = P.emit("pe", lambda h, qt=qt, kc=kc, dc=dc:
                                 h.matmul(ps_row4[:, kc * 512:(kc + 1) * 512],
                                          Q_fm[:, dc * T + qt * 128:dc * T + (qt + 1) * 128],
                                          Q_fm[:, dc * T + kc * 512:dc * T + (kc + 1) * 512],
                                          start=(dc == 0), stop=(dc == NC - 1)),
                                 waits=[prev_exp], inc="pe")
            nm = P.emit("dve", lambda h: h.reduce_max(negm[:, 0:1], ps_row4[:, :], AX.X,
                                                      negate=True), waits=[mml], inc="dve")
            nm = P.emit("dve", lambda h: h.tensor_scalar_mul(negm[:, 1:2], negm[:, 0:1],
                                                             1.0 / 32.0), inc="dve")
            ex = P.emit("act", lambda h: h.activation(A_bf[:, :], ps_row4[:, :], AF.Exp,
                                                      scale=1.0 / 32.0, bias=negm[:, 1:2],
                                                      accum_out=negm[:, 0:1]),
                        waits=[nm], inc="act")
            rr = P.emit("dve", lambda h: h.reciprocal(negm[:, 1:2], negm[:, 0:1]),
                        waits=[ex], inc="dve")
            an = P.emit("act", lambda h: h.activation(A_bf[:, :], A_bf[:, :], AF.Copy,
                                                      scale=negm[:, 1:2]),
                        waits=[rr], inc="act")
            prev_exp = an
            at_ev = []
            for kt in range(NT):
                tt = P.emit("pe", lambda h, kt=kt: h.transpose(
                    ps_t[:, (kt % 2) * 1024:(kt % 2) * 1024 + 128],
                    A_bf[:, kt * 128:(kt + 1) * 128], B.identbf[:, :]),
                    waits=[an, at_ev[-2] if len(at_ev) >= 2 else None], inc="pe")
                at_ev.append(P.emit("dve", lambda h, kt=kt: h.tensor_copy(
                    At[:, kt * 128:(kt + 1) * 128],
                    ps_t[:, (kt % 2) * 1024:(kt % 2) * 1024 + 128]), waits=[tt], inc="dve"))
            for dc in range(NC):
                mmo = None
                for kt in range(NT):
                    mmo = P.emit("pe", lambda h, dc=dc, kt=kt, qt=qt:
                                 h.matmul(ps_o[:, 128:256],
                                          Q_tm[:, kt * D + dc * 128:kt * D + (dc + 1) * 128],
                                          At[:, kt * 128:(kt + 1) * 128],
                                          start=(kt == 0), stop=(kt == NT - 1)),
                                 waits=[at_ev[kt], prev_out], inc="pe")
                prev_out = P.emit("dve", lambda h, dc=dc, qt=qt: h.tensor_tensor(
                    B.x_fm[:, dc * T + qt * 128:dc * T + (qt + 1) * 128],
                    ps_o[:, 128:256],
                    B.x_fm[:, dc * T + qt * 128:dc * T + (qt + 1) * 128], ALU.add),
                    waits=[mmo], inc="dve")
            P.flush()
        P.barrier()
        P.flush()


def emit_l2_ffn(B, gate_tm, gtok):
    P, nc = B.P, B.nc
    with contextlib.ExitStack() as st:
        def sb(name, shape, dtype=BF16):
            return st.enter_context(nc.sbuf_tensor(name, list(shape), dtype))
        WT = sb("l2f_WT", (128, NC * D))
        UT = sb("l2f_UT", (128, NC * 128))
        VT = [sb(f"l2f_VT{e}", (16, D)) for e in range(8)]
        vnat = sb("l2f_vnat", (128, 16))
        g16 = sb("l2f_g16", (128, NT * 8), F32)
        t_bf = sb("l2f_t", (128, 128))
        tpT = [sb(f"l2f_tpT{e}", (16, 128)) for e in range(8)]
        ps_a = st.enter_context(nc.psum_tensor("l2fpsa", [128, 512], F32))
        ps_b = st.enter_context(nc.psum_tensor("l2fpsb", [128, 512], F32))
        ps_c = st.enter_context(nc.psum_tensor("l2fpsc", [16, 512], BF16))
        wbf, wtok = B.prep_cast("L2_ffn_weight")
        ubf, utok = B.prep_cast("L2_ffn_lora_U")
        V = B.ins["L2_ffn_lora_V"]
        wt_tok = None
        for ic in range(NC):
            wt_tok = P.dma("sp", lambda h, ic=ic: h.dma_start(
                WT[:, ic * D:(ic + 1) * D], wbf[:, ic * 128:(ic + 1) * 128], transpose=True),
                "l2wT", waits=[wtok])
        ut_tok = None
        for ic in range(NC):
            ut_tok = P.dma("sp", lambda h, ic=ic: h.dma_start(
                UT[:, ic * 128:(ic + 1) * 128], ubf[:, ic * 128:(ic + 1) * 128], transpose=True),
                "l2uT", waits=[utok])
        vprev = None
        for e in range(8):
            for oc in range(NC):
                vd = P.dma("pool", lambda h, e=e, oc=oc: h.dma_start(
                    vnat[:, :], V[e, oc * 128:(oc + 1) * 128, :]), "l2v", waits=[vprev])
                vt = P.emit("pe", lambda h, e=e, oc=oc: h.transpose(
                    ps_c[:, (oc % 4) * 128:(oc % 4) * 128 + 128], vnat[:, 0:16],
                    B.identbf[:, :]),
                    waits=[vd, B.consts_tok, vprev], inc="pe")
                vprev = P.emit("dve", lambda h, e=e, oc=oc: h.tensor_copy(
                    VT[e][0:16, oc * 128:(oc + 1) * 128],
                    ps_c[0:16, (oc % 4) * 128:(oc % 4) * 128 + 128]), waits=[vt], inc="dve")
        gs = P.emit("dve", lambda h: h.tensor_scalar_mul(g16[:, :], gate_tm[:, :], 1.0 / 16.0),
                    waits=[gtok], inc="dve")
        P.flush()
        prev_out = None
        for t in range(NT):
            mmt = None
            for ic in range(NC):
                mmt = P.emit("pe", lambda h, t=t, ic=ic:
                             h.matmul(ps_a[:, 0:128],
                                      B.xn_bf[:, ic * T + t * 128:ic * T + (t + 1) * 128],
                                      UT[:, ic * 128:(ic + 1) * 128],
                                      start=(ic == 0), stop=(ic == NC - 1)),
                             waits=[ut_tok, vprev, gs, prev_out], inc="pe")
            te = P.emit("act", lambda h, t=t: h.activation(t_bf[:, :], ps_a[:, 0:128], AF.Copy),
                        waits=[mmt], inc="act")
            tp_ev = []
            for e in range(8):
                sc = P.emit("dve", lambda h, t=t, e=e: h.tensor_scalar_mul(
                    t_bf[:, e * 16:(e + 1) * 16], t_bf[:, e * 16:(e + 1) * 16],
                    g16[:, t * 8 + e:t * 8 + e + 1]), waits=[te], inc="dve")
                tt = P.emit("pe", lambda h, e=e: h.transpose(
                    ps_c[:, (e % 4) * 128:(e % 4) * 128 + 128], t_bf[:, e * 16:(e + 1) * 16],
                    B.identbf[:, :]),
                    waits=[sc, tp_ev[-1] if tp_ev else None], inc="pe")
                tp_ev.append(P.emit("dve", lambda h, e=e: h.tensor_copy(
                    tpT[e][0:16, :], ps_c[0:16, (e % 4) * 128:(e % 4) * 128 + 128]),
                    waits=[tt], inc="dve"))
            for oc in range(NC):
                mmo = None
                for ic in range(NC):
                    mmo = P.emit("pe", lambda h, t=t, oc=oc, ic=ic:
                                 h.matmul(ps_b[:, 0:128],
                                          WT[:, ic * D + oc * 128:ic * D + (oc + 1) * 128],
                                          B.xn_bf[:, ic * T + t * 128:ic * T + (t + 1) * 128],
                                          start=(ic == 0), stop=False),
                                 waits=[wt_tok, tp_ev[-1], prev_out], inc="pe")
                for e in range(8):
                    mmo = P.emit("pe", lambda h, t=t, oc=oc, e=e:
                                 h.matmul(ps_b[:, 0:128],
                                          VT[e][0:16, oc * 128:(oc + 1) * 128],
                                          tpT[e][0:16, :],
                                          start=False, stop=(e == 7)),
                                 inc="pe")
                prev_out = P.emit("dve", lambda h, t=t, oc=oc: h.tensor_tensor(
                    B.x_fm[:, oc * T + t * 128:oc * T + (t + 1) * 128],
                    ps_b[:, 0:128],
                    B.x_fm[:, oc * T + t * 128:oc * T + (t + 1) * 128], ALU.add),
                    waits=[mmo], inc="dve")
            P.flush()
        P.barrier()
        P.flush()


def emit_l3(B):
    P, nc = B.P, B.nc
    with contextlib.ExitStack() as rst:
        bounce = [rst.enter_context(nc.sbuf_tensor(f"l3bounce{i}", [128, 1024], BF16))
                  for i in range(2)]
        attn_prep = []
        for nm in ("L3_attn_Wq", "L3_attn_Wk", "L3_attn_Wv", "L3_attn_Wo"):
            _, tks = B.prep_transposed_dram(nm, bounce)
            attn_prep += tks
        for nm in ("L3_ffn_w1", "L3_ffn_w2"):
            B.prep_cast(nm)
        probs_tm = rst.enter_context(nc.sbuf_tensor("l3_probs", [128, NT * 8], F32))
        gT = rst.enter_context(nc.sbuf_tensor("l3_gT", [128, NC * 8 + 8], F32))
        gamcol = rst.enter_context(nc.sbuf_tensor("l3_gam", [128, 2 * NC], F32))
        grow_neg = rst.enter_context(nc.sbuf_tensor("l3_grow", [1, 16], F32))
        B.scr81 = rst.enter_context(nc.sbuf_tensor("l3_scr81", [128, 8], F32))
        B.layernorm(want_tok_stats=True, rows_stack=rst, l3=True)
        gtok = l3_gate_probs(B, probs_tm, gT, gamcol, grow_neg)
        cw_tm = probs_tm
        cwtok = l3_top2(B, probs_tm)
        emit_l3_attn(B, attn_prep)
        R2B = rst.enter_context(nc.sbuf_tensor("l3_R2B", [128, T], F32))
        with nc.psum_tensor("l3_bps", [128, 512], F32) as bps:
            ev = []
            for j in range(4):
                mmb = P.emit("pe", lambda h, j=j: h.matmul(
                    bps[:, :], B.ones_r32[:, :], B.r2row[:, j * 512:(j + 1) * 512],
                    start=True, stop=True), waits=[ev[-1] if ev else None], inc="pe")
                ev.append(P.emit("act", lambda h, j=j: h.activation(
                    R2B[:, j * 512:(j + 1) * 512], bps[:, :], AF.Copy),
                    waits=[mmb], inc="act"))
            xn2tok = None
            for c in range(NC):
                P.emit("dve", lambda h, c=c: h.tensor_tensor(
                    B.x_fm[:, c * T:(c + 1) * T], B.xn_bf[:, c * T:(c + 1) * T],
                    B.x_fm[:, c * T:(c + 1) * T], ALU.add), waits=[ev[-1]], inc="dve")
                P.emit("dve", lambda h, c=c: h.tensor_mul(
                    B.xn_bf[:, c * T:(c + 1) * T], B.xn_bf[:, c * T:(c + 1) * T],
                    R2B[:, :]), inc="dve")
                xn2tok = P.emit("dve", lambda h, c=c: h.tensor_scalar(
                    B.xn_bf[:, c * T:(c + 1) * T], B.xn_bf[:, c * T:(c + 1) * T],
                    gamcol[:, c:c + 1], gamcol[:, NC + c:NC + c + 1],
                    ALU.mult, ALU.add), inc="dve")
            P.flush()
        emit_l3_ffn(B, cw_tm, cwtok, xn2tok)
        P.barrier()
        P.flush()


def l3_gate_probs(B, probs_tm, gT, gamcol, grow_neg):
    P, nc = B.P, B.nc
    g_ap = B.ins["L3_ffn_gate_w"]
    gam = B.ins["L3_ffn_gamma"]
    bet = B.ins["L3_ffn_beta"]
    with contextlib.ExitStack() as st:
        gb_nat = st.enter_context(nc.sbuf_tensor("l3_gbnat", [NC, 2 * 128], F32))
        Bb = st.enter_context(nc.sbuf_tensor("l3_Bb", [128, 8], F32))
        with nc.psum_tensor("l3psg", [128, 1024], F32) as ps_g:
            d1 = P.dma("pool", lambda h: h.dma_start(
                gT[:, 0:NC * 8].rearrange("p (c e) -> p c e", e=8),
                g_ap.rearrange("(c p) e -> p c e", p=128)), "gA")
            d2 = P.dma("pool", lambda h: h.dma_start(
                gb_nat[:, 0:128], gam.rearrange("(c p) -> c p", p=128)), "gA")
            d3 = P.dma("pool", lambda h: h.dma_start(
                gb_nat[:, 128:256], bet.rearrange("(c p) -> c p", p=128)), "gA")
            t1 = P.emit("pe", lambda h: h.transpose(ps_g[:, 512:512 + NC], gb_nat[:, 0:128],
                                                    B.ident32[0:NC, 0:NC]),
                        waits=[d3, B.consts_tok], inc="pe")
            t2 = P.emit("pe", lambda h: h.transpose(ps_g[:, 512 + NC:512 + 2 * NC],
                                                    gb_nat[:, 128:256],
                                                    B.ident32[0:NC, 0:NC]),
                        inc="pe")
            gb_tok = P.emit("act", lambda h: h.activation(gamcol[:, :],
                                                          ps_g[:, 512:512 + 2 * NC], AF.Copy),
                            waits=[t1, t2], inc="act")
            mm = None
            for c in range(NC):
                mm = P.emit("pe", lambda h, c=c: h.matmul(
                    ps_g[0:1, 0:8], gb_nat[c:c + 1, 128:256] if False else gamcol[:, NC + c:NC + c + 1],
                    gT[:, c * 8:(c + 1) * 8], start=(c == 0), stop=(c == NC - 1)),
                    waits=[d3, gb_tok], inc="pe")
            bB = P.emit("act", lambda h: h.activation(grow_neg[0:1, 8:16], ps_g[0:1, 0:8],
                                                      AF.Copy), waits=[mm], inc="act")
            gs = None
            for c in range(NC):
                gs = P.emit("dve", lambda h, c=c: h.tensor_scalar_mul(
                    gT[:, c * 8:(c + 1) * 8], gT[:, c * 8:(c + 1) * 8],
                    gamcol[:, c:c + 1]), waits=[gb_tok, bB], inc="dve")
            mm = None
            for c in range(NC):
                mm = P.emit("pe", lambda h, c=c: h.matmul(
                    ps_g[0:1, 8:16], B.ones_c32[:, :], gT[:, c * 8:(c + 1) * 8],
                    start=(c == 0), stop=(c == NC - 1)), waits=[gs, bB], inc="pe")
            gv = P.emit("dve", lambda h: h.tensor_scalar_mul(grow_neg[0:1, 0:8],
                                                             ps_g[0:1, 8:16], -1.0),
                        waits=[mm], inc="dve")
            mmb = P.emit("pe", lambda h: h.matmul(ps_g[:, 16:24], B.ones_r32[:, :],
                                                  grow_neg[0:1, 8:16], start=True, stop=True),
                         waits=[bB, gv], inc="pe")
            bb_tok = P.emit("act", lambda h: h.activation(Bb[:, :], ps_g[:, 16:24], AF.Copy),
                            waits=[mmb], inc="act")
            toks = []
            for t in range(NT):
                mmt = None
                for c in range(NC):
                    mmt = P.emit("pe", lambda h, t=t, c=c:
                                 h.matmul(ps_g[:, (t % 2) * 512:(t % 2) * 512 + 8],
                                          B.x_fm[:, c * T + t * 128:c * T + (t + 1) * 128],
                                          gT[:, c * 8:(c + 1) * 8],
                                          start=(c == 0), stop=False),
                                 waits=[gv, bb_tok, toks[t - 2] if t >= 2 else None], inc="pe")
                mmt = P.emit("pe", lambda h, t=t:
                             h.matmul(ps_g[:, (t % 2) * 512:(t % 2) * 512 + 8],
                                      B.murow[:, t * 128:(t + 1) * 128],
                                      grow_neg[0:1, 0:8], start=False, stop=True),
                             inc="pe")
                tl = P.emit("act", lambda h, t=t: h.activation(
                    probs_tm[:, t * 8:(t + 1) * 8], ps_g[:, (t % 2) * 512:(t % 2) * 512 + 8],
                    AF.Copy, scale=B.rr_t[:, t:t + 1]), waits=[mmt], inc="act")
                tl = P.emit("dve", lambda h, t=t: h.tensor_add(
                    probs_tm[:, t * 8:(t + 1) * 8], probs_tm[:, t * 8:(t + 1) * 8],
                    Bb[:, :]), waits=[tl], inc="dve")
                nm = P.emit("dve", lambda h, t=t: h.reduce_max(
                    B.scr81[:, 0:1], probs_tm[:, t * 8:(t + 1) * 8], AX.X, negate=True),
                    waits=[tl], inc="dve")
                te = P.emit("act", lambda h, t=t: h.activation(
                    probs_tm[:, t * 8:(t + 1) * 8], probs_tm[:, t * 8:(t + 1) * 8], AF.Exp,
                    bias=B.scr81[:, 0:1], accum_out=B.scr81[:, 1:2]), waits=[nm], inc="act")
                tr = P.emit("dve", lambda h: h.reciprocal(B.scr81[:, 2:3], B.scr81[:, 1:2]),
                            waits=[te], inc="dve")
                toks.append(P.emit("act", lambda h, t=t: h.activation(
                    probs_tm[:, t * 8:(t + 1) * 8], probs_tm[:, t * 8:(t + 1) * 8], AF.Copy,
                    scale=B.scr81[:, 2:3]), waits=[tr], inc="act"))
            P.flush()
    return toks[-1]


def l3_top2(B, probs_tm):
    P, nc = B.P, B.nc
    with contextlib.ExitStack() as st:
        m1 = st.enter_context(nc.sbuf_tensor("l3_m1", [128, 4], F32))
        pp = st.enter_context(nc.sbuf_tensor("l3_pp", [128, 8], F32))
        last = None
        for t in range(NT):
            pt = probs_tm[:, t * 8:(t + 1) * 8]
            P.emit("dve", lambda h, pt=pt: h.reduce_max(m1[:, 0:1], pt, AX.X), inc="dve")
            P.emit("dve", lambda h, pt=pt: h.tensor_scalar(pp[:, :], pt, m1[:, 0:1], None,
                                                           ALU.is_ge), inc="dve")
            P.emit("dve", lambda h, pt=pt: h.scalar_tensor_tensor(
                pp[:, :], pp[:, :], -1e30, pt, ALU.mult, ALU.add), inc="dve")
            P.emit("dve", lambda h: h.reduce_max(m1[:, 1:2], pp[:, :], AX.X), inc="dve")
            P.emit("dve", lambda h, pt=pt: h.tensor_scalar(pp[:, :], pt, m1[:, 1:2], None,
                                                           ALU.is_ge), inc="dve")
            P.emit("dve", lambda h: h.tensor_add(m1[:, 2:3], m1[:, 0:1], m1[:, 1:2]),
                   inc="dve")
            P.emit("dve", lambda h: h.reciprocal(m1[:, 3:4], m1[:, 2:3]), inc="dve")
            P.emit("dve", lambda h, pt=pt: h.tensor_mul(pt, pt, pp[:, :]), inc="dve")
            last = P.emit("dve", lambda h, pt=pt: h.tensor_scalar_mul(pt, pt, m1[:, 3:4]),
                          inc="dve")
        P.flush()
    return last


def emit_l3_attn(B, attn_prep):
    P, nc = B.P, B.nc
    POSC = 4
    with contextlib.ExitStack() as st:
        def sb(name, shape, dtype=BF16):
            return st.enter_context(nc.sbuf_tensor(name, list(shape), dtype))
        WkT = sb("l3_WkT", (128, 2 * 512))
        WvT = sb("l3_WvT", (128, 2 * 512))
        wqb = [sb(f"l3_wqb{i}", (128, 512)) for i in range(2)]
        wob = [sb(f"l3_wob{i}", (128, 1024)) for i in range(2)]
        k_all = sb("l3_kall", (128, 2 * T))
        v_all = sb("l3_vall", (128, 4 * NT * 64))
        qr_bf = sb("l3_qr", (128, T))
        A_bf = sb("l3_A", (128, T))
        At = sb("l3_At", (128, NT * 128))
        att_all = sb("l3_attall", (128, NC * T))
        fscr = sb("l3_fscr", (128, T), F32)
        Ew = sb("l3_Ew", (128, 2048), F32)
        trig = sb("l3_trig", (128, 3 * 16 + 4), F32)
        pint = sb("l3_pint", (128, 1), mybir.dt.int32)
        trow = sb("l3_trow", (1, 3 * 16), F32)
        rotK = sb("l3_rotK", (64, 3 * 64), F32)
        rotT = sb("l3_rotT", (128, 64))
        r_all = sb("l3_rall", (128, NT + 2), F32)
        rrow = sb("l3_rrow", (1, 128), F32)
        ps_s = st.enter_context(nc.psum_tensor("l3pss", [128, 1024], F32))
        ps_t = st.enter_context(nc.psum_tensor("l3pst", [128, 2048], BF16))
        ps_z = st.enter_context(nc.psum_tensor("l3psz", [64, 512], F32))
        ps_rb = st.enter_context(nc.psum_tensor("l3psrb", [64, 512], F32))
        ps_row = st.enter_context(nc.psum_tensor("l3psrow", [1, 512], F32))

        qTS = B.prep_aps["L3_attn_Wq_T"]; qtk = B.prep_toks["L3_attn_Wq_T"]
        kTS = B.prep_aps["L3_attn_Wk_T"]; ktk = B.prep_toks["L3_attn_Wk_T"]
        vTS = B.prep_aps["L3_attn_Wv_T"]; vtk = B.prep_toks["L3_attn_Wv_T"]
        oTS = B.prep_aps["L3_attn_Wo_T"]; otk = B.prep_toks["L3_attn_Wo_T"]

        ewt = P.emit("pool", lambda h: h.iota(Ew[:, :], [[-1, 2048]], base=1920,
                                              channel_multiplier=1,
                                              allow_small_or_imprecise_dtypes=True),
                     inc="pool")
        ewt = P.emit("pool", lambda h: h.affine_select(
            Ew[:, :], Ew[:, :], pattern=[[-1, 2048]], compare_op=ALU.is_ge,
            fill=-1e33, base=1920, channel_multiplier=1), waits=[ewt], inc="pool")
        d1 = P.dma("pool", lambda h: h.dma_start(
            trow[0:1, 0:16], B.ins["L3_attn_q_phase_angle"].rearrange("(o d) -> o d", o=1)),
            "gA")
        d2 = P.dma("pool", lambda h: h.dma_start(
            trow[0:1, 32:48], B.ins["L3_attn_alibi_log_slopes"].rearrange("(o d) -> o d", o=1)),
            "gA")
        tg = P.emit("dve", lambda h: h.tensor_scalar_add(trow[0:1, 16:32], trow[0:1, 0:16],
                                                         math.pi / 2), waits=[d2], inc="dve")
        tg = P.emit("act", lambda h: h.activation(trow[0:1, 16:32], trow[0:1, 16:32], AF.Sin),
                    waits=[tg], inc="act")
        tg = P.emit("act", lambda h: h.activation(trow[0:1, 0:16], trow[0:1, 0:16], AF.Sin),
                    inc="act")
        tg = P.emit("act", lambda h: h.activation(trow[0:1, 32:48], trow[0:1, 32:48], AF.Exp),
                    inc="act")
        mmb = P.emit("pe", lambda h: h.matmul(ps_s[:, 0:48], B.ones_r32[:, :], trow[0:1, 0:48],
                                              start=True, stop=True),
                     waits=[tg, B.consts_tok], inc="pe")
        trig_tok = P.emit("act", lambda h: h.activation(trig[:, 0:48], ps_s[:, 0:48], AF.Copy),
                          waits=[mmb], inc="act")
        kt0 = P.emit("pool", lambda h: h.iota(pint[:, :], [[0, 1]], base=0,
                                              channel_multiplier=1),
                     waits=[ewt], inc="pool")
        kt0 = P.emit("dve", lambda h: h.tensor_single_scalar(pint[:, :], pint[:, :],
                                                             1, ALU.bitwise_and),
                     waits=[kt0], inc="dve")
        kt0 = P.emit("dve", lambda h: h.tensor_copy(trig[:, 49:50], pint[:, :]),
                     inc="dve")
        kt0 = P.emit("dve", lambda h: h.tensor_scalar(trig[:, 50:51], trig[:, 49:50], -1.0, 1.0,
                                                      ALU.mult, ALU.add), inc="dve")
        km = P.emit("pool", lambda h: h.memset(rotK[:, 0:64], 0.0), waits=[kt0], inc="pool")
        km = P.emit("pool", lambda h: h.affine_select(
            rotK[:, 0:64], rotK[:, 0:64], pattern=[[-1, 64]], compare_op=ALU.not_equal,
            fill=1.0, base=1, channel_multiplier=1), waits=[km], inc="pool")
        km = P.emit("pool", lambda h: h.memset(rotK[:, 64:128], 0.0), waits=[km], inc="pool")
        km = P.emit("pool", lambda h: h.affine_select(
            rotK[:, 64:128], rotK[:, 64:128], pattern=[[-1, 64]], compare_op=ALU.not_equal,
            fill=1.0, base=-1, channel_multiplier=1), waits=[km], inc="pool")
        km2 = P.emit("dve", lambda h: h.tensor_scalar_mul(rotK[:, 0:64], rotK[:, 0:64],
                                                          trig[0:64, 50:51]),
                     waits=[km], inc="dve")
        km2 = P.emit("dve", lambda h: h.tensor_scalar_mul(rotK[:, 64:128], rotK[:, 64:128],
                                                          trig[0:64, 49:50]), inc="dve")
        km2 = P.emit("dve", lambda h: h.tensor_sub(rotK[:, 0:64], rotK[:, 0:64],
                                                   rotK[:, 64:128]), inc="dve")
        wk_tok = P.dma("sp", lambda h: h.dma_start(WkT[:, :], kTS[:, :]), "l3wk", waits=ktk)
        wv_tok = P.dma("sp", lambda h: h.dma_start(WvT[:, :], vTS[:, :]), "l3wv", waits=vtk)
        kv_ev = []
        for a in range(4):
            blk, off = a // 2, (a % 2) * 64
            for jh in range(2):
                mmt = None
                for c in range(POSC):
                    for j2 in range(2):
                        mmt = P.emit("pe", lambda h, a=a, jh=jh, c=c, j2=j2, blk=blk, off=off:
                                     h.matmul(ps_s[0:64, j2 * 512:(j2 + 1) * 512],
                                              WkT[:, blk * 512 + c * 128 + off:
                                                  blk * 512 + c * 128 + off + 64],
                                              B.xn_bf[:, (4 + c) * T + jh * 1024 + j2 * 512:
                                                      (4 + c) * T + jh * 1024 + (j2 + 1) * 512],
                                              start=(c == 0), stop=(c == POSC - 1)),
                                     waits=[wk_tok, trig_tok,
                                            kv_ev[-1] if kv_ev else None], inc="pe")
                kv_ev.append(P.emit("act", lambda h, a=a, jh=jh: h.activation(
                    k_all[(a % 2) * 64:(a % 2) * 64 + 64,
                          (a // 2) * T + jh * 1024:(a // 2) * T + (jh + 1) * 1024],
                    ps_s[0:64, :], AF.Copy), waits=[mmt], inc="act"))
            for t in range(NT):
                mmt = None
                for c in range(POSC):
                    mmt = P.emit("pe", lambda h, a=a, t=t, c=c, blk=blk, off=off:
                                 h.matmul(ps_s[:, 0:64],
                                          B.xn_bf[:, c * T + t * 128:c * T + (t + 1) * 128],
                                          WvT[:, blk * 512 + c * 128 + off:
                                              blk * 512 + c * 128 + off + 64],
                                          start=(c == 0), stop=(c == POSC - 1)),
                                 waits=[wv_tok, kv_ev[-1]], inc="pe")
                kv_ev.append(P.emit("act", lambda h, a=a, t=t: h.activation(
                    v_all[:, a * NT * 64 + t * 64:a * NT * 64 + (t + 1) * 64],
                    ps_s[:, 0:64], AF.Copy), waits=[mmt], inc="act"))
        P.flush()
        wfree = {}
        pfq = P.dma("sp", lambda h: h.dma_start(wqb[0][:, :], qTS[:, 0:512]), "l3q0",
                    waits=qtk)
        prev_tail = kv_ev[-1]
        for h_ in range(16):
            a = h_ // 4
            off = (a % 2) * 64
            hp, hh = h_ // 2, h_ % 2
            bi = hp % 2
            if hh == 0:
                curq = pfq
                if hp < 7:
                    pfq = P.dma("sp", lambda h, hp=hp: h.dma_start(
                        wqb[(hp + 1) % 2][:, :], qTS[:, (hp + 1) * 512:(hp + 2) * 512]),
                        "l3q" + str((hp + 1) % 2), waits=[wfree.get(("q", (hp + 1) % 2))])
            q_ev = []
            for jh in range(2):
                mmt = None
                for c in range(POSC):
                    for j2 in range(2):
                        mmt = P.emit("pe", lambda h, jh=jh, c=c, j2=j2, bi=bi, hh=hh:
                                     h.matmul(ps_s[0:64, j2 * 512:(j2 + 1) * 512],
                                              wqb[bi][:, c * 128 + hh * 64:c * 128 + hh * 64 + 64],
                                              B.xn_bf[:, (4 + c) * T + jh * 1024 + j2 * 512:
                                                      (4 + c) * T + jh * 1024 + (j2 + 1) * 512],
                                              start=(c == 0), stop=(c == POSC - 1)),
                                     waits=[curq, prev_tail,
                                            q_ev[-1] if q_ev else None], inc="pe")
                q_ev.append(P.emit("act", lambda h, jh=jh, off=off: h.activation(
                    qr_bf[off:off + 64, jh * 1024:(jh + 1) * 1024], ps_s[0:64, :], AF.Copy,
                    scale=0.125), waits=[mmt], inc="act"))
            if hh == 1:
                wfree[("q", bi)] = q_ev[-1]
            rt = P.emit("dve", lambda h, h_=h_: h.tensor_scalar_mul(
                rotK[:, 128:192], B.ident32[0:64, 0:64], trig[0:64, 16 + h_:17 + h_]),
                waits=[km2, trig_tok], inc="dve")
            rt = P.emit("dve", lambda h, h_=h_: h.scalar_tensor_tensor(
                rotK[:, 128:192], rotK[:, 0:64], trig[0:64, h_:h_ + 1],
                rotK[:, 128:192], ALU.mult, ALU.add), inc="dve")
            rt = P.emit("act", lambda h, off=off: h.activation(
                rotT[off:off + 64, :], rotK[:, 128:192], AF.Copy), waits=[rt], inc="act")
            qr_ev = []
            for jh in range(2):
                for j2 in range(2):
                    cols = (jh * 1024 + j2 * 512, jh * 1024 + (j2 + 1) * 512)
                    mmt = P.emit("pe", lambda h, cols=cols, j2=j2, off=off:
                                 h.matmul(ps_s[0:64, j2 * 512:(j2 + 1) * 512],
                                          rotT[off:off + 64, :],
                                          qr_bf[off:off + 64, cols[0]:cols[1]],
                                          start=True, stop=True),
                                 waits=[rt, q_ev[-1], qr_ev[-1] if qr_ev else None], inc="pe")
                    qr_ev.append(P.emit("act", lambda h, cols=cols, j2=j2, off=off: h.activation(
                        qr_bf[off:off + 64, cols[0]:cols[1]], ps_s[0:64, j2 * 512:(j2 + 1) * 512],
                        AF.Copy), waits=[mmt], inc="act"))
            prev_exp = qr_ev[-1]
            prev_z = None
            c_out = (h_ // 2) * T
            p0 = (h_ % 2) * 64
            for qt in range(NT):
                K_len = (qt + 1) * 128
                nch = (K_len + 511) // 512
                stoks = []
                for cc in range(nch):
                    c0 = cc * 512
                    cl = min(512, K_len - c0)
                    pcol = (cc % 2) * 512
                    mms = P.emit("pe", lambda h, qt=qt, c0=c0, cl=cl, pcol=pcol, off=off, a=a:
                                 h.matmul(ps_s[:, pcol:pcol + cl],
                                          qr_bf[off:off + 64, qt * 128:(qt + 1) * 128],
                                          k_all[off:off + 64,
                                                (a // 2) * T + c0:(a // 2) * T + c0 + cl],
                                          start=True, stop=True),
                                 waits=[prev_exp, prev_z,
                                        stoks[cc - 2] if cc >= 2 else None], inc="pe")
                    stoks.append(P.emit("dve", lambda h, qt=qt, c0=c0, cl=cl, pcol=pcol, h_=h_:
                                        h.scalar_tensor_tensor(
                                            fscr[:, c0:c0 + cl],
                                            Ew[:, 1920 - 128 * qt + c0:1920 - 128 * qt + c0 + cl],
                                            trig[:, 32 + h_:33 + h_],
                                            ps_s[:, pcol:pcol + cl],
                                            ALU.mult, ALU.add), waits=[mms, ewt], inc="dve"))
                nm = P.emit("dve", lambda h, qt=qt, K_len=K_len: h.reduce_max(
                    r_all[:, NT:NT + 1], fscr[:, 0:K_len], AX.X, negate=True),
                    waits=stoks, inc="dve")
                ex = P.emit("act", lambda h, qt=qt, K_len=K_len: h.activation(
                    A_bf[:, 0:K_len], fscr[:, 0:K_len], AF.Exp, bias=r_all[:, NT:NT + 1],
                    accum_out=r_all[:, NT + 1:NT + 2]), waits=[nm], inc="act")
                prev_exp = ex
                rr = P.emit("dve", lambda h, qt=qt: h.reciprocal(
                    r_all[:, qt:qt + 1], r_all[:, NT + 1:NT + 2]), waits=[ex], inc="dve")
                at_ev = []
                for kt in range(qt + 1):
                    tt = P.emit("pe", lambda h, kt=kt: h.transpose(
                        ps_t[:, (kt % 2) * 1024:(kt % 2) * 1024 + 128],
                        A_bf[:, kt * 128:(kt + 1) * 128], B.identbf[:, :]),
                        waits=[ex, at_ev[-2] if len(at_ev) >= 2 else None], inc="pe")
                    at_ev.append(P.emit("dve", lambda h, kt=kt: h.tensor_copy(
                        At[:, kt * 128:(kt + 1) * 128],
                        ps_t[:, (kt % 2) * 1024:(kt % 2) * 1024 + 128]),
                        waits=[tt], inc="dve"))
                mmz = None
                for kt in range(qt + 1):
                    mmz = P.emit("pe", lambda h, kt=kt, a=a, qt=qt:
                                 h.matmul(ps_z[:, 128:256],
                                          v_all[:, a * NT * 64 + kt * 64:a * NT * 64 + (kt + 1) * 64],
                                          At[:, kt * 128:(kt + 1) * 128],
                                          start=(kt == 0), stop=(kt == qt)),
                                 waits=[at_ev[kt], prev_z], inc="pe")
                ze = P.emit("act", lambda h, qt=qt: h.activation(
                    fscr[0:64, 0:128], ps_z[:, 128:256], AF.Copy), waits=[mmz], inc="act")
                tr2 = P.emit("pe", lambda h, qt=qt: h.transpose(
                    ps_row[:, 0:128], r_all[:, qt:qt + 1], B.ident32[:, :]),
                    waits=[rr, prev_z], inc="pe")
                re = P.emit("act", lambda h: h.activation(rrow[0:1, :], ps_row[0:1, 0:128],
                                                          AF.Copy), waits=[tr2], inc="act")
                mrb = P.emit("pe", lambda h: h.matmul(ps_rb[0:64, 0:128], B.ones_r32[:, 0:64],
                                                      rrow[0:1, :], start=True, stop=True),
                             waits=[re], inc="pe")
                prev_z = P.emit("dve", lambda h, qt=qt, c_out=c_out, p0=p0: h.tensor_tensor(
                    att_all[p0:p0 + 64, c_out + qt * 128:c_out + (qt + 1) * 128],
                    fscr[0:64, 0:128], ps_rb[0:64, 0:128], ALU.mult),
                    waits=[ze, mrb], inc="dve")
            prev_tail = prev_z
            P.flush()
        ev_prev = {}
        pfo = P.dma("sp", lambda h: h.dma_start(wob[0][:, :], oTS[:, 0:1024]), "l3o0",
                    waits=otk)
        for o in range(NC):
            bo = o % 2
            curo = pfo
            if o < NC - 1:
                pfo = P.dma("sp", lambda h, o=o: h.dma_start(
                    wob[(o + 1) % 2][:, :], oTS[:, (o + 1) * 1024:(o + 2) * 1024]),
                    "l3o" + str((o + 1) % 2), waits=[wfree.get(("o", (o + 1) % 2))])
            mmt = {}
            for jh in range(2):
                m = None
                for c in range(NC):
                    for j2 in range(2):
                        m = P.emit("pe", lambda h, o=o, c=c, jh=jh, j2=j2, bo=bo:
                                   h.matmul(ps_s[:, j2 * 512:(j2 + 1) * 512],
                                            wob[bo][:, c * 128:(c + 1) * 128],
                                            att_all[:, c * T + jh * 1024 + j2 * 512:
                                                    c * T + jh * 1024 + (j2 + 1) * 512],
                                            start=(c == 0), stop=(c == NC - 1)),
                                   waits=[curo, prev_tail, ev_prev.get((o, jh - 1)) or
                                          ev_prev.get((o - 1, 1))], inc="pe")
                mmt[jh] = m
                ev_prev[(o, jh)] = P.emit(
                    "dve", lambda h, o=o, jh=jh: h.tensor_tensor(
                        B.x_fm[:, o * T + jh * 1024:o * T + (jh + 1) * 1024],
                        ps_s[:, :],
                        B.x_fm[:, o * T + jh * 1024:o * T + (jh + 1) * 1024], ALU.add),
                    waits=[m], inc="dve")
            wfree[("o", bo)] = mmt[1]
        P.barrier()
        P.flush()


def emit_l3_ffn(B, cw_tm, cwtok, xn2tok):
    P, nc = B.P, B.nc
    import os
    NTOK = 512
    NTL = int(os.environ.get("L3NTL", T // NTOK))
    w1bf, c1tok = B.prep_cast("L3_ffn_w1")
    w2bf, c2tok = B.prep_cast("L3_ffn_w2")
    with contextlib.ExitStack() as st:
        def sb(name, shape, dtype=BF16):
            return st.enter_context(nc.sbuf_tensor(name, list(shape), dtype))
        h_bf = sb("l3f_h", (128, 32 * NTOK))
        w1t = [sb(f"l3f_w1t{i}", (128, NC * 128)) for i in range(2)]
        w2t = [sb(f"l3f_w2t{i}", (128, 32 * 128)) for i in range(2)]
        b1c = sb("l3f_b1c", (128, 8 * 32), F32)
        b1n = sb("l3f_b1n", (32, 128), F32)
        b2bf = sb("l3f_b2", (8, 1024))
        cwT = sb("l3f_cwT", (8, T))
        CW = sb("l3f_CW", (128, NTOK), F32)
        srowx = sb("l3f_srowx", (1, NTOK), F32)
        tscr = sb("l3f_tscr", (128, NTOK), F32)
        ps = [st.enter_context(nc.psum_tensor(f"l3fps{i}", [128, NTOK], F32)) for i in range(5)]
        ps_c = st.enter_context(nc.psum_tensor("l3fpsc", [8, 512], F32))
        for _e in ("sp", "pool", "dve", "act"):
            P.emit(_e, lambda h: h.nop(), waits=attn_prep_of(B))
        bt = None
        for e in range(8):
            bt = P.dma("pool", lambda h, e=e: h.dma_start(
                b1n[:, :], B.ins["L3_ffn_b1"][e, :].rearrange("(c p) -> c p", p=128)),
                "gA", waits=[bt])
            t1 = P.emit("pe", lambda h, e=e: h.transpose(
                ps[4][:, 0:32], b1n[:, :], B.ident32[0:32, 0:32]),
                waits=[bt, B.consts_tok], inc="pe")
            bt = P.emit("act", lambda h, e=e: h.activation(
                b1c[:, e * 32:(e + 1) * 32], ps[4][:, 0:32], AF.Copy), waits=[t1], inc="act")
        b2t = P.dma("pool", lambda h: h.dma_start(b2bf[:, :], B.ins["L3_ffn_b2"]), "gA")
        cwe = None
        for t in range(NT):
            t1 = P.emit("pe", lambda h, t=t: h.transpose(
                ps_c[:, (t % 2) * 128:(t % 2) * 128 + 128], cw_tm[:, t * 8:(t + 1) * 8],
                B.ident32[:, :]), waits=[cwtok, cwe], inc="pe")
            cwe = P.emit("act", lambda h, t=t: h.activation(
                cwT[0:8, t * 128:(t + 1) * 128], ps_c[0:8, (t % 2) * 128:(t % 2) * 128 + 128],
                AF.Copy), waits=[t1], inc="act")
        P.flush()
        prev_y = cwe
        for nt in range(NTL):
            n0 = nt * NTOK
            for e in range(8):
                rx = []
                for tt4 in range(NTOK // 128):
                    tglob = (n0 + tt4 * 128) // 128
                    mmr = P.emit("pe", lambda h, e=e, tglob=tglob, tt4=tt4: h.matmul(
                        ps_c[0:1, (tt4 % 2) * 128:(tt4 % 2) * 128 + 128],
                        cw_tm[:, tglob * 8 + e:tglob * 8 + e + 1], B.ident32[:, :],
                        start=True, stop=True),
                        waits=[cwtok, prev_y, rx[-1] if rx else None], inc="pe")
                    rx.append(P.emit("act", lambda h, tt4=tt4: h.activation(
                        srowx[:, tt4 * 128:(tt4 + 1) * 128],
                        ps_c[0:1, (tt4 % 2) * 128:(tt4 % 2) * 128 + 128], AF.Copy),
                        waits=[mmr], inc="act"))
                mmb = P.emit("pe", lambda h: h.matmul(ps[4][:, :], B.ones_r32[:, :],
                                                      srowx[:, :], start=True, stop=True),
                             waits=[rx[-1]], inc="pe")
                cwb = P.emit("act", lambda h: h.activation(CW[:, :], ps[4][:, :], AF.Copy),
                             waits=[mmb], inc="act")
                h_toks = []
                ld1 = {}

                def load_w1(hc):
                    bi = hc % 2
                    ld1[hc] = P.dma("sp", lambda h, e=e, hc=hc, bi=bi: h.dma_start(
                        w1t[bi][:, :].rearrange("p (c o) -> p c o", o=128),
                        w1bf[e * 1024:(e + 1) * 1024, hc * 128:(hc + 1) * 128].rearrange(
                            "(c p) o -> p c o", p=128)),
                        "l3w1" + str(bi),
                        waits=[c1tok if (nt == 0 and e == 0 and hc < 2) else None,
                               h_toks[hc - 2] if hc >= 2 else prev_y])
                load_w1(0)
                load_w1(1)
                for hc in range(32):
                    bi = hc % 2
                    psx = ps[hc % 2]
                    mm = None
                    for dc in range(NC):
                        mm = P.emit("pe", lambda h, hc=hc, dc=dc, psx=psx, bi=bi, n0=n0:
                                    h.matmul(psx[:, :],
                                             w1t[bi][:, dc * 128:(dc + 1) * 128],
                                             B.xn_bf[:, dc * T + n0:dc * T + n0 + NTOK],
                                             start=(dc == 0), stop=(dc == NC - 1)),
                                    waits=[ld1[hc], cwb,
                                           h_toks[hc - 2] if hc >= 2 else None], inc="pe")
                    h_toks.append(P.emit("act", lambda h, e=e, hc=hc, psx=psx: h.activation(
                        h_bf[:, hc * NTOK:(hc + 1) * NTOK], psx[:, :], AF.Gelu,
                        bias=b1c[:, e * 32 + hc:e * 32 + hc + 1]),
                        waits=[mm, bt], inc="act"))
                    if hc + 2 < 32:
                        load_w1(hc + 2)
                y_toks = []
                ld2 = {}

                def load_w2(dc):
                    bi = dc % 2
                    ld2[dc] = P.dma("sp", lambda h, e=e, dc=dc, bi=bi: h.dma_start(
                        w2t[bi][:, :].rearrange("p (c o) -> p c o", o=128),
                        w2bf[e * 4096:(e + 1) * 4096, dc * 128:(dc + 1) * 128].rearrange(
                            "(c p) o -> p c o", p=128)),
                        "l3w2" + str(bi),
                        waits=[c2tok if (nt == 0 and e == 0 and dc < 2) else None,
                               y_toks[dc - 2][0] if dc >= 2 else None])
                load_w2(0)
                load_w2(1)
                for dc in range(NC):
                    bi = dc % 2
                    psx = ps[2 + dc % 2]
                    mm = None
                    for hc in range(32):
                        mm = P.emit("pe", lambda h, dc=dc, hc=hc, psx=psx, bi=bi:
                                    h.matmul(psx[:, :],
                                             w2t[bi][:, hc * 128:(hc + 1) * 128],
                                             h_bf[:, hc * NTOK:(hc + 1) * NTOK],
                                             start=(hc == 0), stop=(hc == 31)),
                                    waits=[ld2[dc], h_toks[-1],
                                           y_toks[dc - 2][1] if dc >= 2 else None], inc="pe")
                    tv = P.emit("dve", lambda h, dc=dc, psx=psx: h.tensor_tensor(
                        tscr[:, :], psx[:, :], CW[:, :], ALU.mult), waits=[mm], inc="dve")
                    ta = P.emit("dve", lambda h, dc=dc, n0=n0: h.tensor_tensor(
                        B.x_fm[:, dc * T + n0:dc * T + n0 + NTOK], tscr[:, :],
                        B.x_fm[:, dc * T + n0:dc * T + n0 + NTOK], ALU.add), inc="dve")
                    y_toks.append((mm, ta))
                    if dc + 2 < NC:
                        load_w2(dc + 2)
                prev_y = y_toks[-1][1]
            for dc in range(NC):
                mmb2 = P.emit("pe", lambda h, dc=dc, n0=n0: h.matmul(
                    ps[4][:, :], b2bf[0:8, dc * 128:(dc + 1) * 128],
                    cwT[0:8, n0:n0 + NTOK], start=True, stop=True),
                    waits=[b2t, cwe, prev_y], inc="pe")
                prev_y = P.emit("dve", lambda h, dc=dc, n0=n0: h.tensor_tensor(
                    B.x_fm[:, dc * T + n0:dc * T + n0 + NTOK], ps[4][:, :],
                    B.x_fm[:, dc * T + n0:dc * T + n0 + NTOK], ALU.add),
                    waits=[mmb2], inc="dve")
            P.flush()
        P.barrier()
        P.flush()


def attn_prep_of(B):
    toks = []
    for nm in ("L3_attn_Wq_T", "L3_attn_Wk_T", "L3_attn_Wv_T", "L3_attn_Wo_T"):
        toks += B.prep_toks.get(nm, [])
    return toks


def emit_layer(B, li):
    [emit_l0, emit_l1, emit_l2, emit_l3][li](B)


def build_program(layers):
    B = Build(layers)
    B.load_x()
    for li in layers:
        emit_layer(B, li)
    B.store_x()
    B.P.flush()
    return B


def kernel(**inputs):
    x = np.ascontiguousarray(np.asarray(inputs["L0_attn_x"], dtype=np.float32))
    xs = [np.ascontiguousarray(x[b]) for b in range(NCORE)]
    for layers in LAYER_GROUPS:
        Bd = build_program(layers)
        in_maps = []
        for b in range(NCORE):
            m = {"x_in": xs[b]}
            for li in layers:
                for k in LAYER_INPUTS[li]:
                    m[k] = np.ascontiguousarray(np.asarray(inputs[k], dtype=np.float32))
            in_maps.append(m)
        res = run_bass_kernel_spmd(Bd.nc, in_maps, core_ids=list(range(NCORE)))
        xs = [res.results[b]["x_out"] for b in range(NCORE)]
    return np.stack(xs, axis=0).astype(np.float32)
